# version 14
# baseline (speedup 1.0000x reference)
import contextlib
import numpy as np
import ml_dtypes
import concourse.bass as bass
import concourse.mybir as mybir
from concourse.bass_utils import run_bass_kernel_spmd

F32 = mybir.dt.float32
BF16 = mybir.dt.bfloat16
F16 = mybir.dt.float16
U16 = mybir.dt.uint16
AF = mybir.ActivationFunctionType
ALU = mybir.AluOpType
bfdt = ml_dtypes.bfloat16

D = 2048
NQH, NKV, HD = 32, 8, 64
SK = 1536
SOWN = 1024
HALO = 512
II = 5632
NEG = -1e30

DEBUG_TAPS = False
import os
PHASES = int(os.environ.get("KPH", "9"))


class _StopEmit(Exception):
    pass


_SKIP_TRACK = {"wait_ge", "nop", "drain", "maybe_wait_ge", "sem_inc", "wait_op"}


class _T:

    def __init__(self, eng):
        self._e = eng
        self.last = None

    def __getattr__(self, k):
        f = getattr(self._e, k)
        if callable(f) and not k.startswith("_"):
            def g(*a, _f=f, _k=k, **kw):
                r = _f(*a, **kw)
                if _k not in _SKIP_TRACK and r is not None:
                    self.last = r
                return r
            return g
        return f


def _wrap_block(block):
    class W:
        def __getattr__(self, name):
            dec = getattr(block, name)

            def mydec(fn):
                def inner(eng):
                    return fn(_T(eng))
                inner.__name__ = getattr(fn, "__name__", "_")
                return dec(inner)
            return mydec
    return W()


def _perm_d():
    h = np.arange(D)
    f = (h % 4) * 512 + (h // 8) * 2 + ((h // 4) % 2)
    s = np.empty(D, np.int64)
    s[f] = h
    return s, f


def _perm_i():
    i = np.arange(II)
    f = (i % 4) * 1408 + (i // 8) * 2 + ((i // 4) % 2)
    s = np.empty(II, np.int64)
    s[f] = i
    return s, f


class Ctr:
    def __init__(self):
        self.v = 0

    def inc(self, n=1):
        self.v += n
        return self.v


def _m3(ap, a):
    return ap.rearrange("(a p) c -> p a c", p=128)


def _s3(ap, a):
    return ap.rearrange("p (a c) -> p a c", a=a)


def build_nc():
    try:
        return _build_nc_inner()
    except _StopEmit as e:
        return e.nc


def _build_nc_inner():
    nc = bass.Bass("TRN2", target_bir_lowering=False, debug=False)

    xT = nc.dram_tensor("xT", [D, SK], F32, kind="ExternalInput").ap()
    wqkvT = nc.dram_tensor("wqkvT", [D, 3072], F32, kind="ExternalInput").ap()
    woutT = nc.dram_tensor("woutT", [D, D], F32, kind="ExternalInput").ap()
    wqpp = nc.dram_tensor("wqpp", [128, 1], F32, kind="ExternalInput").ap()
    wkpp = nc.dram_tensor("wkpp", [128, 1], F32, kind="ExternalInput").ap()
    pk16 = nc.dram_tensor("pk16", [512, 11264], U16, kind="ExternalInput").ap()
    screp = nc.dram_tensor("screp", [512, 11264], F16, kind="ExternalInput").ap()
    dpk16 = nc.dram_tensor("dpk16", [1408, D], U16, kind="ExternalInput").ap()
    dscrep = nc.dram_tensor("dscrep", [1408, D], F16, kind="ExternalInput").ap()
    masks_in = nc.dram_tensor("masks_in", [128, 8 * 640], BF16, kind="ExternalInput").ap()
    cos_in = nc.dram_tensor("cos_in", [128, SK], F32, kind="ExternalInput").ap()
    sin_in = nc.dram_tensor("sin_in", [128, SK], F32, kind="ExternalInput").ap()
    eye_in = nc.dram_tensor("eye_in", [128, 128], BF16, kind="ExternalInput").ap()
    bd_in = nc.dram_tensor("bd_in", [128, 128], BF16, kind="ExternalInput").ap()
    ones_in = nc.dram_tensor("ones_in", [128, 128], BF16, kind="ExternalInput").ap()
    consts_in = nc.dram_tensor("consts_in", [128, 8], F32, kind="ExternalInput").ap()

    o_out = nc.dram_tensor("o_out", [D, SOWN], F32, kind="ExternalOutput").ap()
    sk_ = "ExternalOutput" if DEBUG_TAPS else "Internal"
    sT_dram = nc.dram_tensor("sT_dram", [D, SOWN], F32, kind=sk_).ap()
    wdn_dram = nc.dram_tensor("wdn_dram", [II, D], BF16, kind=sk_).ap()
    if DEBUG_TAPS:
        xn_dram = nc.dram_tensor("xn_dram", [D, SK], BF16, kind="ExternalOutput").ap()
        at_dram = nc.dram_tensor("at_dram", [D, SOWN], BF16, kind="ExternalOutput").ap()
        h_dram = nc.dram_tensor("h_dram", [II, SOWN], BF16, kind="ExternalOutput").ap()

    es = contextlib.ExitStack()
    _stacks = []
    with es:
        E = es.enter_context

        semL = E(nc.semaphore("semL"))
        semG = E(nc.semaphore("semG"))
        semV = E(nc.semaphore("semV"))
        semS = E(nc.semaphore("semS"))
        semT = E(nc.semaphore("semT"))
        semB = E(nc.semaphore("semB"))
        semO = E(nc.semaphore("semO"))
        cL, cG, cV, cS, cT, cB, cO = (Ctr() for _ in range(7))

        block = _wrap_block(E(nc.Block()))

        def barrier():
            tgt = cB.inc(5)
            for name in ("sync", "gpsimd", "vector", "scalar", "tensor"):
                @getattr(block, name)
                def _(eng, name=name):
                    if name != "sync":
                        eng.drain()
                    eng.nop().then_inc(semB, 1)
                    eng.wait_ge(semB, tgt)


        def maybe_stop(n):
            if PHASES <= n:
                @block.sync
                def _(sync):
                    sync.wait_ge(semO, cO.v)
                for s in reversed(_stacks):
                    try:
                        s.close()
                    except Exception:
                        pass
                err = _StopEmit()
                err.nc = nc
                raise err

        consts = E(nc.sbuf_tensor("consts", [128, 8], F32))
        eye = E(nc.sbuf_tensor("eye", [128, 128], BF16))
        bd64 = E(nc.sbuf_tensor("bd64", [128, 128], BF16))
        ones = E(nc.sbuf_tensor("ones", [128, 128], BF16))
        swq = E(nc.sbuf_tensor("swq", [128, 1], F32))
        swk = E(nc.sbuf_tensor("swk", [128, 1], F32))

        xnT = E(nc.sbuf_tensor("xnT", [128, 16 * SOWN], BF16))

        atn = contextlib.ExitStack()
        _stacks.append(atn)

        att = contextlib.ExitStack()
        _stacks.append(att)
        EA = att.enter_context
        masks = EA(nc.sbuf_tensor("masks", [128, 8 * 640], BF16))
        qhat = EA(nc.sbuf_tensor("qhat", [128, 16 * SOWN], BF16))
        khat = EA(nc.sbuf_tensor("khat", [128, 8 * SK], BF16))
        vst = EA(nc.sbuf_tensor("vst", [128, 12 * 512], BF16))
        rope_sc = contextlib.ExitStack()
        _stacks.append(rope_sc)
        ER = rope_sc.enter_context
        cosT = ER(nc.sbuf_tensor("cosT", [128, SK], F32))
        sinT = ER(nc.sbuf_tensor("sinT", [128, SK], F32))
        xnH = ER(nc.sbuf_tensor("xnH", [128, 16 * HALO], BF16))

        @block.sync
        def _(sync):
            for dst, src in ((consts[:], consts_in), (eye[:], eye_in),
                             (bd64[:], bd_in), (ones[:], ones_in),
                             (swq[:], wqpp), (swk[:], wkpp),
                             (masks[:], masks_in), (cosT[:], cos_in),
                             (sinT[:], sin_in)):
                sync.dma_start(out=dst, in_=src).then_inc(semL, 16)
                cL.inc(16)

        if os.environ.get("KBAR"):
            barrier()
        maybe_stop(0)

        c_half = consts[:, 0:1]
        c_qeps = consts[:, 1:2]
        c_keps = consts[:, 2:3]
        c_lneps = consts[:, 3:4]
        c_feps = consts[:, 4:5]

        _sid_P1_ln = nc.enter_named_scope("P1_ln", False)[0]
        ln = contextlib.ExitStack()
        _stacks.append(ln)
        EL = ln.enter_context
        xin = [EL(nc.sbuf_tensor(f"xin{p}", [128, 16 * 256], F32, side="right")) for p in range(2)]
        xbb_ = EL(nc.sbuf_tensor("xbb", [128, 16 * 256], BF16, side="right"))
        xsq_ = EL(nc.sbuf_tensor("xsq", [128, 16 * 256], BF16, side="right"))
        xbb = [xbb_, xbb_]
        xsq = [xsq_, xsq_]
        xm2 = [EL(nc.sbuf_tensor(f"xm2{q}", [128, 256], F32, side="right")) for q in range(4)]
        mtile = EL(nc.sbuf_tensor("mtile", [128, 256], F32, side="right"))
        m2tile = EL(nc.sbuf_tensor("m2tile", [128, 256], F32, side="right"))
        vtile = EL(nc.sbuf_tensor("vtile", [128, 256], F32, side="right"))
        stile = EL(nc.sbuf_tensor("stile", [128, 256], F32, side="right"))
        rtile = EL(nc.sbuf_tensor("rtile", [128, 256], F32, side="right"))
        ps_mb = EL(nc.psum_tensor("ps_mb", [128, 1024], F32))
        ps_sq = EL(nc.psum_tensor("ps_sq", [128, 1024], F32))

        ln_load_done, ln_ev_done, ln_sq_done, ln_mm_done = [], [], [], []
        ln_var_done, ln_std_done = [], []

        KLNC = int(os.environ.get("KLNC", "6"))
        KLNS = int(os.environ.get("KLNS", "9"))
        for c in range(KLNC):
            p = c % 2

            @block.sync
            def _(sync, c=c, p=p):
                if c >= 2:
                    sync.wait_ge(semV, ln_ev_done[c - 2])
                sync.dma_start(out=_s3(xin[p][:], 16),
                               in_=_m3(xT[:, 256 * c:256 * (c + 1)], 16)).then_inc(semL, 16)
                cL.inc(16)
            ln_load_done.append(cL.v)

            @block.vector
            def _(vector, c=c, p=p):
                if KLNS < 2:
                    return
                vector.wait_ge(semL, ln_load_done[c])
                if c >= 1:
                    vector.wait_ge(semT, ln_mm_done[c - 1])
                for rt in range(16):
                    sl = slice(rt * 256, (rt + 1) * 256)
                    vector.tensor_copy(xbb[p][:, sl], xin[p][:, sl])
                for rt in range(16):
                    sl = slice(rt * 256, (rt + 1) * 256)
                    vector.tensor_tensor(xsq[p][:, sl], xbb[p][:, sl], xbb[p][:, sl], ALU.mult)
                vector.last.then_inc(semV, 1)
                cV.inc()
            ln_sq_done.append(cV.v)

            @block.tensor
            def _(tensor, c=c, p=p):
                if KLNS < 3:
                    return
                tensor.wait_ge(semV, ln_sq_done[c])
                if c >= 2:
                    tensor.wait_ge(semV, ln_var_done[c - 2])
                psl = slice(p * 512, p * 512 + 256)
                for rt in range(16):
                    sl = slice(rt * 256, (rt + 1) * 256)
                    tensor.matmul(ps_mb[:, psl], ones[:, :], xbb[p][:, sl],
                                  start=(rt == 0), stop=(rt == 15))
                for rt in range(16):
                    sl = slice(rt * 256, (rt + 1) * 256)
                    tensor.matmul(ps_sq[:, psl], ones[:, :], xsq[p][:, sl],
                                  start=(rt == 0), stop=(rt == 15))
                tensor.last.then_inc(semT, 1)
                cT.inc()
            ln_mm_done.append(cT.v)

            @block.vector
            def _(vector, c=c, p=p):
                if KLNS < 4:
                    return
                vector.wait_ge(semT, ln_mm_done[c])
                psl = slice(p * 512, p * 512 + 256)
                vector.tensor_scalar(mtile[:], ps_mb[:, psl], 1.0 / D, None, ALU.mult)
                vector.drain()
                vector.tensor_tensor(m2tile[:], mtile[:], mtile[:], ALU.mult)
                vector.drain()
                vector.scalar_tensor_tensor(vtile[:], ps_sq[:, psl], 1.0 / D, m2tile[:],
                                            ALU.mult, ALU.subtract)
                vector.last.then_inc(semV, 1)
                cV.inc()
            ln_var_done.append(cV.v)

            @block.scalar
            def _(scalar, c=c):
                if KLNS < 5:
                    return
                scalar.wait_ge(semV, ln_var_done[c])
                scalar.activation(stile[:], vtile[:], AF.Sqrt, bias=c_lneps, scale=1.0)
                scalar.last.then_inc(semS, 1)
                cS.inc()
            ln_std_done.append(cS.v)

            @block.vector
            def _(vector, c=c, p=p):
                if KLNS < 6:
                    return
                vector.wait_ge(semS, ln_std_done[c])
                vector.reciprocal(rtile[:], stile[:])
                vector.drain()
                def _dst(rt):
                    if c < 2:
                        return xnH[:, rt * HALO + 256 * c: rt * HALO + 256 * (c + 1)]
                    j0 = 256 * c - 512
                    return xnT[:, rt * SOWN + j0: rt * SOWN + j0 + 256]
                for rt0 in range(0, 16, 4):
                    for q in range(4):
                        sl = slice((rt0 + q) * 256, (rt0 + q + 1) * 256)
                        vector.tensor_tensor(xm2[q][:], xin[p][:, sl], mtile[:], ALU.subtract)
                    for q in range(4):
                        vector.tensor_tensor(_dst(rt0 + q), xm2[q][:], rtile[:], ALU.mult)
                vector.last.then_inc(semV, 1)
                cV.inc()
            ln_ev_done.append(cV.v)

        barrier()
        if DEBUG_TAPS:
            @block.sync
            def _(sync):
                for rt in range(16):
                    sync.dma_start(out=xn_dram[128 * rt:128 * (rt + 1), 0:HALO],
                                   in_=xnH[:, rt * HALO:(rt + 1) * HALO]).then_inc(semO, 16)
                    cO.inc(16)
                    sync.dma_start(out=xn_dram[128 * rt:128 * (rt + 1), HALO:SK],
                                   in_=xnT[:, rt * SOWN:(rt + 1) * SOWN]).then_inc(semO, 16)
                    cO.inc(16)
        ln.close()
        maybe_stop(1)


        nc.leave_named_scope("P1_ln", _sid_P1_ln, False)
        _sid_P2_qkv = nc.enter_named_scope("P2_qkv", False)[0]
        qk = contextlib.ExitStack()
        _stacks.append(qk)
        EQ = qk.enter_context
        wcol = [EQ(nc.sbuf_tensor(f"wcol{p}", [128, 16 * 128], BF16, side="right")) for p in range(2)]
        sqscr = EQ(nc.sbuf_tensor("sqscr", [128, 512], BF16, side="right"))
        qcb = EQ(nc.sbuf_tensor("qcb", [128, 512], BF16, side="right"))
        sqs = EQ(nc.sbuf_tensor("sqs", [128, 512], F32, side="right"))
        rqs = EQ(nc.sbuf_tensor("rqs", [128, 512], F32, side="right"))
        qns = EQ(nc.sbuf_tensor("qns", [128, 512], F32, side="right"))
        rots = EQ(nc.sbuf_tensor("rots", [128, 512], F32, side="right"))
        t1s = EQ(nc.sbuf_tensor("t1s", [128, 512], F32, side="right"))
        t2s = EQ(nc.sbuf_tensor("t2s", [128, 512], F32, side="right"))
        vtmps = EQ(nc.sbuf_tensor("vtmps", [128, 512], BF16, side="right"))
        ps_qkv = EQ(nc.psum_tensor("ps_qkv", [128, 1024], F32))
        ps_ss = EQ(nc.psum_tensor("ps_ss", [128, 1024], F32))
        ps_vt = EQ(nc.psum_tensor("ps_vt", [128, 1024], BF16))

        dd = contextlib.ExitStack()
        _stacks.append(dd)
        ED = dd.enter_context
        HC = 1024
        dpkb = [ED(nc.sbuf_tensor(f"dpkb{p}", [128, HC], U16, side="right")) for p in range(2)]
        dscb = [ED(nc.sbuf_tensor(f"dscb{p}", [128, HC], F16, side="right")) for p in range(2)]
        ddv = [ED(nc.sbuf_tensor(f"ddv{k}", [128, HC], U16, side="right")) for k in range(4)]
        dds = [ED(nc.sbuf_tensor(f"dds{k}", [128, HC], U16, side="right")) for k in range(4)]
        dscq = ED(nc.sbuf_tensor("dscq", [128, HC], F16, side="right"))
        ddec = [[ED(nc.sbuf_tensor(f"ddec{p}_{k}", [128, HC], BF16, side="right")) for k in range(4)]
                for p in range(2)]
        dd_load, dd_dec, dd_st = ({} for _ in range(3))

        def emit_dd_load(hc):
            rt, hf = hc // 2, hc % 2
            pb = hc % 2

            @block.sync
            def _(sync, rt=rt, hf=hf, pb=pb, hc=hc):
                if hc >= 2:
                    sync.wait_ge(semV, dd_dec[hc - 2])
                sync.dma_start(out=dpkb[pb][:],
                               in_=dpk16[128 * rt:128 * (rt + 1), HC * hf:HC * (hf + 1)]
                               ).then_inc(semL, 16)
                cL.inc(16)
                sync.dma_start(out=dscb[pb][:],
                               in_=dscrep[128 * rt:128 * (rt + 1), HC * hf:HC * (hf + 1)]
                               ).then_inc(semL, 16)
                cL.inc(16)
            dd_load[hc] = cL.v

        def emit_dd_decode(hc):
            rt, hf = hc // 2, hc % 2
            pb = hc % 2

            @block.vector
            def _(vector, rt=rt, hf=hf, pb=pb, hc=hc):
                vector.wait_ge(semL, dd_load[hc])
                if hc >= 2:
                    vector.wait_ge(semO, dd_st[hc - 2])
                W = dpkb[pb][:]
                vector.tensor_scalar(dscq[:], dscb[pb][:], 16384.0, None, ALU.mult)
                for kk in range(4):
                    sh = 9 - 4 * kk
                    if sh >= 0:
                        vector.tensor_scalar(ddv[kk][:], W, sh, 7 << 9,
                                             ALU.logical_shift_left, ALU.bitwise_and)
                    else:
                        vector.tensor_scalar(ddv[kk][:], W, -sh, 7 << 9,
                                             ALU.logical_shift_right, ALU.bitwise_and)
                for kk in range(4):
                    vector.tensor_scalar(dds[kk][:], W, 4 * kk + 3, 15,
                                         ALU.logical_shift_right, ALU.logical_shift_left)
                for kk in range(4):
                    vector.tensor_tensor(ddv[kk][:], ddv[kk][:], dds[kk][:], ALU.add)
                for kk in range(4):
                    vector.tensor_tensor(ddec[pb][kk][:], ddv[kk][:].bitcast(F16),
                                         dscq[:], ALU.mult)
                vector.last.then_inc(semV, 1)
                cV.inc()
            dd_dec[hc] = cV.v

            @block.sync
            def _(sync, rt=rt, hf=hf, pb=pb, hc=hc):
                sync.wait_ge(semV, dd_dec[hc])
                for kk in range(4):
                    it = 11 * kk + rt
                    sync.dma_start(out=wdn_dram[128 * it:128 * (it + 1), HC * hf:HC * (hf + 1)],
                                   in_=ddec[pb][kk][:]).then_inc(semO, 16)
                    cO.inc(16)
            dd_st[hc] = cO.v

        def rhs_xn(kt, ch):
            if ch == 0:
                return xnH[:, kt * HALO: kt * HALO + 512]
            j0 = 512 * (ch - 1)
            return xnT[:, kt * SOWN + j0: kt * SOWN + j0 + 512]

        qk_wload = {}
        qk_mm = {}
        qk_ev = {}
        qk_ssmm = {}
        qk_sqrt = {}
        qk_vtp = {}
        qk_vtmp = {}

        seq2 = []
        for ot in range(24):
            for ch in ([0, 1, 2] if ot >= 16 else [1, 2]):
                seq2.append((ot, ch))

        prev_on_parity = {}
        prev_sqrt = None
        prev2_sqrt = None
        prev_ev = None
        prev_v = None
        for ot in range(24):
            if ot >= 2:
                emit_dd_decode(ot - 2)
            if ot < 22:
                emit_dd_load(ot)

            @block.gpsimd
            def _(gpsimd, ot=ot):
                if ot >= 2:
                    gpsimd.wait_ge(semT, qk_mm[(ot - 2, 2)])
                gpsimd.dma_start(
                    out=_s3(wcol[ot % 2][:], 16),
                    in_=_m3(wqkvT[:, 128 * ot:128 * (ot + 1)], 16)).then_inc(semG, 16)
                cG.inc(16)
            qk_wload[ot] = cG.v

            for ch in ([0, 1, 2] if ot >= 16 else [1, 2]):
                pos = seq2.index((ot, ch))
                par = pos % 2
                psl = slice(par * 512, (par + 1) * 512)

                @block.tensor
                def _(tensor, ot=ot, ch=ch, par=par, psl=psl):
                    tensor.wait_ge(semG, qk_wload[ot])
                    if par in prev_on_parity:
                        tensor.wait_ge(semV, qk_ev[prev_on_parity[par]])
                    for kt in range(16):
                        tensor.matmul(ps_qkv[:, psl], wcol[ot % 2][:, kt * 128:(kt + 1) * 128],
                                      rhs_xn(kt, ch), start=(kt == 0), stop=(kt == 15))
                    tensor.last.then_inc(semT, 1)
                    cT.inc()
                qk_mm[(ot, ch)] = cT.v
                prev_on_parity[par] = (ot, ch)

                if ot < 20:
                    is_q = ot < 16

                    @block.vector
                    def _(vector, ot=ot, ch=ch, psl=psl):
                        vector.wait_ge(semT, qk_mm[(ot, ch)])
                        vector.tensor_copy(qcb[:], ps_qkv[:, psl])
                        vector.drain()
                        vector.tensor_tensor(sqscr[:], qcb[:], qcb[:], ALU.mult)
                        vector.last.then_inc(semV, 1)
                        cV.inc()
                    sq_ready = cV.v

                    sspar = len(qk_ssmm) % 2
                    sssl = slice(sspar * 512, sspar * 512 + 512)

                    @block.tensor
                    def _(tensor, sq_ready=sq_ready, prev2=prev2_sqrt, sssl=sssl):
                        tensor.wait_ge(semV, sq_ready)
                        if prev2 is not None:
                            tensor.wait_ge(semS, prev2)
                        tensor.matmul(ps_ss[:, sssl], bd64[:, :], sqscr[:], start=True, stop=True)
                        tensor.last.then_inc(semT, 1)
                        cT.inc()
                    qk_ssmm[(ot, ch)] = cT.v

                    @block.scalar
                    def _(scalar, ot=ot, ch=ch, is_q=is_q, sssl=sssl, pv_ev=prev_ev):
                        scalar.wait_ge(semT, qk_ssmm[(ot, ch)])
                        if pv_ev is not None:
                            scalar.wait_ge(semV, pv_ev)
                        scalar.activation(sqs[:], ps_ss[:, sssl], AF.Sqrt,
                                          bias=(c_qeps if is_q else c_keps),
                                          scale=(1.0 if is_q else 1.0 / 64))
                        scalar.last.then_inc(semS, 1)
                        cS.inc()
                    qk_sqrt[(ot, ch)] = cS.v
                    prev2_sqrt = prev_sqrt
                    prev_sqrt = cS.v

                    @block.vector
                    def _(vector, ot=ot, ch=ch, psl=psl, is_q=is_q):
                        vector.wait_ge(semS, qk_sqrt[(ot, ch)])
                        vector.reciprocal(rqs[:], sqs[:])
                        vector.drain()
                        vector.scalar_tensor_tensor(qns[:], ps_qkv[:, psl],
                                                    (swq[:, 0:1] if is_q else swk[:, 0:1]),
                                                    rqs[:], ALU.mult, ALU.mult)
                        vector.drain()
                        for hb in range(2):
                            b0 = hb * 64
                            vector.tensor_scalar(rots[b0:b0 + 32, :], qns[b0 + 32:b0 + 64, :],
                                                 -1.0, None, ALU.mult)
                            vector.tensor_copy(rots[b0 + 32:b0 + 64, :], qns[b0:b0 + 32, :])
                        j0 = 512 * ch
                        cs = cosT[:, j0:j0 + 512]
                        sn = sinT[:, j0:j0 + 512]
                        vector.tensor_tensor(t1s[:], qns[:], cs, ALU.mult)
                        vector.tensor_tensor(t2s[:], rots[:], sn, ALU.mult)
                        vector.drain()
                        if is_q:
                            jq = 512 * (ch - 1)
                            dst = qhat[:, ot * SOWN + jq: ot * SOWN + jq + 512]
                            vector.tensor_tensor(dst, t1s[:], t2s[:], ALU.add)
                        else:
                            gA = 2 * (ot - 16)
                            jk = 512 * ch
                            for (gg, pr) in ((gA, slice(0, 64)), (gA + 1, slice(64, 128))):
                                for half in (slice(0, 64), slice(64, 128)):
                                    vector.tensor_tensor(
                                        khat[half, gg * SK + jk: gg * SK + jk + 512],
                                        t1s[pr, :], t2s[pr, :], ALU.add)
                        vector.last.then_inc(semV, 1)
                        cV.inc()
                    qk_ev[(ot, ch)] = cV.v
                    prev_ev = cV.v
                else:
                    @block.vector
                    def _(vector, ot=ot, ch=ch, psl=psl, prev_v=prev_v):
                        vector.wait_ge(semT, qk_mm[(ot, ch)])
                        if prev_v is not None:
                            vector.wait_ge(semT, qk_vtp[prev_v])
                        vector.tensor_copy(vtmps[:], ps_qkv[:, psl])
                        vector.last.then_inc(semV, 1)
                        cV.inc()
                    qk_vtmp[(ot, ch)] = cV.v

                    @block.tensor
                    def _(tensor, ot=ot, ch=ch):
                        tensor.wait_ge(semV, qk_vtmp[(ot, ch)])
                        for u in range(4):
                            tensor.transpose(ps_vt[:, u * 128:(u + 1) * 128],
                                             vtmps[:, u * 128:(u + 1) * 128], eye[:, :])
                        tensor.last.then_inc(semT, 1)
                        cT.inc()
                    qk_vtp[(ot, ch)] = cT.v
                    prev_v = (ot, ch)

                    @block.vector
                    def _(vector, ot=ot, ch=ch):
                        vector.wait_ge(semT, qk_vtp[(ot, ch)])
                        for u in range(4):
                            ktile = ch * 4 + u
                            dst = vst[:, ktile * 512 + 128 * (ot - 20):
                                      ktile * 512 + 128 * (ot - 20) + 128]
                            vector.tensor_copy(dst, ps_vt[:, u * 128:(u + 1) * 128])
                        vector.last.then_inc(semV, 1)
                        cV.inc()
                    qk_ev[(ot, ch)] = cV.v

        emit_dd_decode(20)
        emit_dd_decode(21)
        barrier()
        dd.close()
        qk.close()
        rope_sc.close()
        attnT = atn.enter_context(nc.sbuf_tensor("attnT", [128, 16 * SOWN], BF16))
        maybe_stop(2)

        nc.leave_named_scope("P2_qkv", _sid_P2_qkv, False)
        _sid_P3_attn = nc.enter_named_scope("P3_attn", False)[0]
        at = contextlib.ExitStack()
        _stacks.append(at)
        ET = at.enter_context
        ptil = [[ET(nc.sbuf_tensor(f"ptil{q}_{p}", [128, 640], BF16, side="right")) for p in range(2)]
                for q in range(2)]
        ptn = [[ET(nc.sbuf_tensor(f"ptn{q}_{p}", [128, 640], BF16, side="right")) for p in range(2)]
               for q in range(2)]
        ptrs = [[ET(nc.sbuf_tensor(f"ptrs{q}_{p}", [128, 640], BF16, side="right")) for p in range(2)]
                for q in range(2)]
        nmx2 = ET(nc.sbuf_tensor("nmx2", [128, 4], F32, side="right"))
        zz2 = ET(nc.sbuf_tensor("zz2", [128, 4], F32, side="right"))
        rz2 = ET(nc.sbuf_tensor("rz2", [128, 2], F32, side="right"))
        strail = ET(nc.sbuf_tensor("strail", [128, 2], F32, side="right"))
        onecol = ET(nc.sbuf_tensor("onecol", [128, 2], F32, side="right"))
        ps_s = [ET(nc.psum_tensor(f"ps_s{p}", [128, 1024], F32)) for p in range(2)]
        ps_pt = [ET(nc.psum_tensor(f"ps_pt{p}", [128, 1024], BF16)) for p in range(2)]
        ps_pv = [ET(nc.psum_tensor(f"ps_pv{p}", [64, 512], F32)) for p in range(2)]

        at_sc, at_mx, at_exp, at_pn, at_tp, at_ptev, at_pv, at_out = ({} for _ in range(8))

        pairs = []
        for t in range(8):
            for h0 in range(0, NQH, 2):
                pairs.append((t, h0))
        NP = len(pairs)
        at_sc, at_mx, at_exp, at_tp, at_pv, at_ptev, vw_end = ({} for _ in range(7))

        def em_scores(i):
            t, h0 = pairs[i]

            @block.tensor
            def _(tensor, t=t, heads=(h0, h0 + 1), i=i):
                if i >= 1:
                    tensor.wait_ge(semS, at_exp[i - 1])
                for j, h in enumerate(heads):
                    g = h >> 2
                    off = 64 * (h & 1)
                    tensor.matmul(ps_s[j][:, 0:512], eye[:, :],
                                  masks[:, 640 * t: 640 * t + 512], start=True, stop=False)
                    tensor.matmul(ps_s[j][:, 512:640], eye[:, :],
                                  masks[:, 640 * t + 512: 640 * t + 640], start=True, stop=False)
                    lhs = qhat[off:off + 64,
                               (h >> 1) * SOWN + 128 * t: (h >> 1) * SOWN + 128 * t + 128]
                    k0 = g * SK + 128 * t
                    tensor.matmul(ps_s[j][:, 0:512], lhs,
                                  khat[off:off + 64, k0:k0 + 512], start=False, stop=False)
                    tensor.matmul(ps_s[j][:, 512:640], lhs,
                                  khat[off:off + 64, k0 + 512:k0 + 640], start=False, stop=True)
                tensor.last.then_inc(semT, 1)
                cT.inc()
            at_sc[i] = cT.v

        def em_tposes(pi):
            @block.tensor
            def _(tensor, pi=pi):
                tensor.wait_ge(semV, vw_end[pi + 1])
                pp = pi % 2
                for j in range(2):
                    for u in range(5):
                        tensor.transpose(ps_pt[j][:, u * 128:(u + 1) * 128],
                                         ptn[pp][j][:, u * 128:(u + 1) * 128], eye[:, :])
                tensor.last.then_inc(semT, 1)
                cT.inc()
            at_tp[pi] = cT.v

        def em_vwork(i):
            has_ptev = (2 <= i <= NP + 1)
            has_max = i < NP
            has_evac = (3 <= i <= NP + 2)
            has_norm = (1 <= i <= NP)

            @block.vector
            def _(vector, i=i, has_ptev=has_ptev, has_max=has_max,
                  has_evac=has_evac, has_norm=has_norm):
                if has_ptev:
                    vector.wait_ge(semT, at_tp[i - 2])
                    if i >= 3:
                        vector.wait_ge(semT, at_pv[i - 3])
                if has_norm:
                    vector.wait_ge(semS, at_exp[i - 1])
                if has_evac:
                    vector.wait_ge(semT, at_pv[i - 3])
                if has_ptev:
                    pp = (i - 2) % 2
                    pe = None
                    for j in range(2):
                        pe = vector.tensor_copy(ptrs[pp][j][:], ps_pt[j][:, 0:640])
                    pe.then_inc(semV, 1)
                    cV.inc()
                    at_ptev[i - 2] = cV.v
                if has_max:
                    vector.wait_ge(semT, at_sc[i])
                    mm = None
                    for j in range(2):
                        jo = (i % 2) * 2 + j
                        mm = vector.reduce_max(nmx2[:, jo:jo + 1], ps_s[j][:, 0:640],
                                               axis=mybir.AxisListType.X, negate=True)
                    mm.then_inc(semV, 1)
                    cV.inc()
                    at_mx[i] = cV.v
                if has_norm:
                    zo = ((i - 1) % 2) * 2
                    vector.reciprocal(rz2[:], zz2[:, zo:zo + 2])
                if has_evac:
                    pj = i - 3
                    t2, h02 = pairs[pj]
                    for j, h in enumerate((h02, h02 + 1)):
                        off = 64 * (h & 1)
                        dst = attnT[off:off + 64,
                                    (h >> 1) * SOWN + 128 * t2: (h >> 1) * SOWN + 128 * t2 + 128]
                        vector.tensor_copy(dst, ps_pv[j][:, 0:128])
                if has_norm:
                    vector.drain()
                    pp = (i - 1) % 2
                    for j in range(2):
                        vector.tensor_scalar(ptn[pp][j][:], ptil[pp][j][:], rz2[:, j:j + 1],
                                             None, ALU.mult)
                if has_norm or has_evac:
                    vector.last.then_inc(semV, 1)
                else:
                    vector.drain()
                    vector.nop().then_inc(semV, 1)
                cV.inc()
            vw_end[i] = cV.v

        def em_exp(i):
            @block.scalar
            def _(scalar, i=i):
                scalar.wait_ge(semV, at_mx[i])
                if i >= 2:
                    scalar.wait_ge(semV, vw_end[i - 1])
                pp = i % 2
                for j in range(2):
                    jo = pp * 2 + j
                    scalar.activation(ptil[pp][j][:], ps_s[j][:, 0:640], AF.Exp,
                                      bias=nmx2[:, jo:jo + 1], scale=1.0,
                                      accum_out=zz2[:, jo:jo + 1])
                scalar.activation(strail[:], nmx2[:, 0:2], AF.Copy)
                scalar.last.then_inc(semS, 1)
                cS.inc()
            at_exp[i] = cS.v

        def em_pv(pj):
            t, h0 = pairs[pj]

            @block.tensor
            def _(tensor, t=t, heads=(h0, h0 + 1), pj=pj):
                tensor.wait_ge(semV, at_ptev[pj])
                if pj >= 1:
                    tensor.wait_ge(semV, vw_end[pj + 2])
                pp = pj % 2
                for j, h in enumerate(heads):
                    g = h >> 2
                    for u in range(5):
                        ktile = t + u
                        tensor.matmul(ps_pv[j][:, 0:128],
                                      vst[:, ktile * 512 + 64 * g: ktile * 512 + 64 * g + 64],
                                      ptrs[pp][j][:, u * 128:(u + 1) * 128],
                                      start=(u == 0), stop=(u == 4))
                tensor.last.then_inc(semT, 1)
                cT.inc()
            at_pv[pj] = cT.v

        for i in range(NP + 3):
            if i < NP:
                em_scores(i)
            if 2 <= i <= NP + 1:
                em_tposes(i - 2)
            em_vwork(i)
            if i < NP:
                em_exp(i)
            if 2 <= i <= NP + 1:
                em_pv(i - 2)

        barrier()
        if DEBUG_TAPS:
            @block.sync
            def _(sync):
                for rt in range(16):
                    sync.dma_start(out=at_dram[128 * rt:128 * (rt + 1), :],
                                   in_=attnT[:, rt * SOWN:(rt + 1) * SOWN]).then_inc(semO, 16)
                    cO.inc(16)
        at.close()
        maybe_stop(3)

        nc.leave_named_scope("P3_attn", _sid_P3_attn, False)
        _sid_P4_wout = nc.enter_named_scope("P4_wout", False)[0]
        wo = contextlib.ExitStack()
        _stacks.append(wo)
        EW = wo.enter_context
        wocol = [EW(nc.sbuf_tensor(f"wocol{p}", [128, 16 * 128], BF16, side="right")) for p in range(2)]
        swo = [EW(nc.sbuf_tensor(f"swo{p}", [128, 512], F32, side="right")) for p in range(2)]
        ps_wo = EW(nc.psum_tensor("ps_wo", [128, 1024], F32))

        wo_load, wo_mm, wo_ev, wo_st = {}, {}, {}, {}
        for dt_ in range(16):
            @block.gpsimd
            def _(gpsimd, dt_=dt_):
                if dt_ >= 2:
                    gpsimd.wait_ge(semT, wo_mm[(dt_ - 2, 1)])
                gpsimd.dma_start(
                    out=_s3(wocol[dt_ % 2][:], 16),
                    in_=_m3(woutT[:, 128 * dt_:128 * (dt_ + 1)], 16)).then_inc(semG, 16)
                cG.inc(16)
            wo_load[dt_] = cG.v

            for c2 in range(2):
                par = c2
                psl = slice(par * 512, (par + 1) * 512)

                @block.tensor
                def _(tensor, dt_=dt_, c2=c2, psl=psl):
                    tensor.wait_ge(semG, wo_load[dt_])
                    prev = (dt_ - 1, c2)
                    if prev in wo_ev:
                        tensor.wait_ge(semV, wo_ev[prev])
                    for kt in range(16):
                        tensor.matmul(ps_wo[:, psl],
                                      wocol[dt_ % 2][:, kt * 128:(kt + 1) * 128],
                                      attnT[:, kt * SOWN + 512 * c2: kt * SOWN + 512 * c2 + 512],
                                      start=(kt == 0), stop=(kt == 15))
                    tensor.last.then_inc(semT, 1)
                    cT.inc()
                wo_mm[(dt_, c2)] = cT.v

                @block.vector
                def _(vector, dt_=dt_, c2=c2, psl=psl):
                    vector.wait_ge(semT, wo_mm[(dt_, c2)])
                    prev = (dt_ - 1, c2)
                    if prev in wo_st:
                        vector.wait_ge(semO, wo_st[prev])
                    vector.tensor_copy(swo[c2][:], ps_wo[:, psl])
                    vector.last.then_inc(semV, 1)
                    cV.inc()
                wo_ev[(dt_, c2)] = cV.v

                @block.sync
                def _(sync, dt_=dt_, c2=c2):
                    sync.wait_ge(semV, wo_ev[(dt_, c2)])
                    sync.dma_start(out=sT_dram[128 * dt_:128 * (dt_ + 1), 512 * c2:512 * (c2 + 1)],
                                   in_=swo[c2][:]).then_inc(semO, 16)
                    cO.inc(16)
                wo_st[(dt_, c2)] = cO.v

        barrier()
        wo.close()
        atn.close()
        att.close()
        maybe_stop(4)

        nc.leave_named_scope("P4_wout", _sid_P4_wout, False)
        _sid_P5_gateup = nc.enter_named_scope("P5_gateup", False)[0]
        gsc = contextlib.ExitStack()
        _stacks.append(gsc)
        g_store = gsc.enter_context(nc.sbuf_tensor("g_store", [128, 44 * SOWN], BF16))
        gu = contextlib.ExitStack()
        _stacks.append(gu)
        EG = gu.enter_context
        dec = [EG(nc.sbuf_tensor(f"dec{p}", [128, 16 * 512], BF16, side="right")) for p in range(2)]
        pkb = [EG(nc.sbuf_tensor(f"pkb{p}", [128, 4 * 512], U16, side="right")) for p in range(2)]
        scb = [EG(nc.sbuf_tensor(f"scb{p}", [128, 4 * 512], F16, side="right")) for p in range(2)]
        dqv = [EG(nc.sbuf_tensor(f"dqv{k}", [128, 2048], U16, side="right")) for k in range(4)]
        dqs = [EG(nc.sbuf_tensor(f"dqs{k}", [128, 2048], U16, side="right")) for k in range(4)]
        scq = EG(nc.sbuf_tensor("scq", [128, 2048], F16, side="right"))
        ps_gu = EG(nc.psum_tensor("ps_gu", [128, 1024], F32))

        gu_load, gu_dq, gu_mm, gu_ev = ({} for _ in range(4))

        for oc in range(22):
            pb = oc % 2

            @block.sync
            def _(sync, oc=oc, pb=pb):
                if oc >= 2:
                    sync.wait_ge(semV, gu_dq[oc - 2])
                sync.dma_start(out=_s3(pkb[pb][:], 4),
                               in_=_m3(pk16[:, 512 * oc:512 * (oc + 1)], 4)).then_inc(semL, 16)
                cL.inc(16)
                sync.dma_start(out=_s3(scb[pb][:], 4),
                               in_=_m3(screp[:, 512 * oc:512 * (oc + 1)], 4)).then_inc(semL, 16)
                cL.inc(16)
            gu_load[oc] = cL.v

            @block.vector
            def _(vector, oc=oc, pb=pb):
                vector.wait_ge(semL, gu_load[oc])
                if oc >= 2:
                    vector.wait_ge(semT, gu_mm[(oc - 2, 3, 1)])
                W = pkb[pb][:, 0:2048]
                vector.tensor_scalar(scq[:], scb[pb][:, 0:2048], 16384.0, None, ALU.mult)
                for kk in range(4):
                    sh = 9 - 4 * kk
                    if sh >= 0:
                        vector.tensor_scalar(dqv[kk][:], W, sh, 7 << 9,
                                             ALU.logical_shift_left, ALU.bitwise_and)
                    else:
                        vector.tensor_scalar(dqv[kk][:], W, -sh, 7 << 9,
                                             ALU.logical_shift_right, ALU.bitwise_and)
                for kk in range(4):
                    vector.tensor_scalar(dqs[kk][:], W, 4 * kk + 3, 15,
                                         ALU.logical_shift_right, ALU.logical_shift_left)
                for kk in range(4):
                    vector.tensor_tensor(dqv[kk][:], dqv[kk][:], dqs[kk][:], ALU.add)
                for kk in range(4):
                    vector.tensor_tensor(
                        dec[oc % 2][:, 2048 * kk: 2048 * (kk + 1)],
                        dqv[kk][:].bitcast(F16), scq[:], ALU.mult)
                vector.last.then_inc(semV, 1)
                cV.inc()
            gu_dq[oc] = cV.v

            for osub in range(4):
                o_glob = 512 * oc + 128 * osub
                for c2 in range(2):
                    par = c2
                    psl = slice(par * 512, (par + 1) * 512)

                    @block.tensor
                    def _(tensor, oc=oc, osub=osub, c2=c2, psl=psl):
                        tensor.wait_ge(semV, gu_dq[oc])
                        seqpos = osub * 2 + c2
                        if seqpos >= 2:
                            pk_ = (oc, (seqpos - 2) // 2, (seqpos - 2) % 2)
                            tensor.wait_ge(semV, gu_ev[pk_])
                        elif oc >= 1:
                            pk_ = (oc - 1, (seqpos + 6) // 2, (seqpos + 6) % 2)
                            tensor.wait_ge(semV, gu_ev[pk_])
                        for ht in range(16):
                            tensor.matmul(
                                ps_gu[:, psl],
                                dec[oc % 2][:, ht * 512 + 128 * osub: ht * 512 + 128 * osub + 128],
                                xnT[:, ht * SOWN + 512 * c2: ht * SOWN + 512 * c2 + 512],
                                start=(ht == 0), stop=(ht == 15))
                        tensor.last.then_inc(semT, 1)
                        cT.inc()
                    gu_mm[(oc, osub, c2)] = cT.v

                    if o_glob < II:
                        gt = o_glob // 128

                        @block.scalar
                        def _(scalar, oc=oc, osub=osub, c2=c2, psl=psl, gt=gt):
                            scalar.wait_ge(semT, gu_mm[(oc, osub, c2)])
                            scalar.activation(
                                g_store[:, gt * SOWN + 512 * c2: gt * SOWN + 512 * c2 + 512],
                                ps_gu[:, psl], AF.Silu)
                            scalar.last.then_inc(semV, 1)
                            cV.inc()
                        gu_ev[(oc, osub, c2)] = cV.v
                    else:
                        ut = (o_glob - II) // 128

                        @block.vector
                        def _(vector, oc=oc, osub=osub, c2=c2, psl=psl, ut=ut):
                            vector.wait_ge(semT, gu_mm[(oc, osub, c2)])
                            sl = slice(ut * SOWN + 512 * c2, ut * SOWN + 512 * c2 + 512)
                            vector.tensor_tensor(g_store[:, sl], g_store[:, sl],
                                                 ps_gu[:, psl], ALU.mult)
                            vector.last.then_inc(semV, 1)
                            cV.inc()
                        gu_ev[(oc, osub, c2)] = cV.v

        barrier()
        if DEBUG_TAPS:
            @block.sync
            def _(sync):
                for gt in range(44):
                    sync.dma_start(out=h_dram[128 * gt:128 * (gt + 1), :],
                                   in_=g_store[:, gt * SOWN:(gt + 1) * SOWN]).then_inc(semO, 16)
                    cO.inc(16)
        gu.close()
        maybe_stop(5)

        nc.leave_named_scope("P5_gateup", _sid_P5_gateup, False)
        _sid_P7_down = nc.enter_named_scope("P7_down", False)[0]
        dn = contextlib.ExitStack()
        _stacks.append(dn)
        EN = dn.enter_context
        wdc = [EN(nc.sbuf_tensor(f"wdc{p}", [128, 44 * 128], BF16, side="right")) for p in range(2)]
        ysq = EN(nc.sbuf_tensor("ysq", [128, 512], BF16, side="right"))
        y_store = EN(nc.sbuf_tensor("y_store", [128, 16 * 512], F32, side="right"))
        srms2 = EN(nc.sbuf_tensor("srms2", [128, 512], F32, side="right"))
        rb2 = EN(nc.sbuf_tensor("rb2", [128, 512], F32, side="right"))
        sxt = [EN(nc.sbuf_tensor(f"sxt{p}", [128, 512], F32, side="right")) for p in range(4)]
        ssT = [EN(nc.sbuf_tensor(f"ssT{p}", [128, 512], F32, side="right")) for p in range(4)]
        comb = EN(nc.sbuf_tensor("comb", [128, 512], F32, side="right"))
        outb = [EN(nc.sbuf_tensor(f"outb{p}", [128, 512], F32, side="right")) for p in range(2)]
        ps_dn = EN(nc.psum_tensor("ps_dn", [128, 1024], F32))
        ps_ss2 = EN(nc.psum_tensor("ps_ss2", [128, 512], F32))

        dn_wload, dn_mm, dn_ysq, dn_ssmm, dn_ev = ({} for _ in range(5))
        dn_xs_load, dn_comb, dn_ost = ({} for _ in range(3))

        for c2 in range(2):
            for ht in range(16):
                pb = ht % 2
                psl = slice(pb * 512, (pb + 1) * 512)

                @block.gpsimd
                def _(gpsimd, c2=c2, ht=ht, pb=pb):
                    prevk = (c2, ht - 2) if ht >= 2 else ((c2 - 1, 14 + ht) if c2 > 0 else None)
                    if prevk is not None and prevk in dn_mm:
                        gpsimd.wait_ge(semT, dn_mm[prevk])
                    gpsimd.dma_start(
                        out=_s3(wdc[pb][:], 44),
                        in_=_m3(wdn_dram[:, 128 * ht:128 * (ht + 1)], 44)).then_inc(semG, 16)
                    cG.inc(16)
                dn_wload[(c2, ht)] = cG.v

                @block.tensor
                def _(tensor, c2=c2, ht=ht, pb=pb, psl=psl):
                    tensor.wait_ge(semG, dn_wload[(c2, ht)])
                    prevk = (c2, ht - 2) if ht >= 2 else ((c2 - 1, 14 + ht) if c2 > 0 else None)
                    if prevk is not None and prevk in dn_ev:
                        tensor.wait_ge(semV, dn_ev[prevk])
                    for kt in range(44):
                        tensor.matmul(ps_dn[:, psl], wdc[pb][:, kt * 128:(kt + 1) * 128],
                                      g_store[:, kt * SOWN + 512 * c2: kt * SOWN + 512 * c2 + 512],
                                      start=(kt == 0), stop=(kt == 43))
                    tensor.last.then_inc(semT, 1)
                    cT.inc()
                dn_mm[(c2, ht)] = cT.v

                @block.scalar
                def _(scalar, c2=c2, ht=ht, psl=psl):
                    scalar.wait_ge(semT, dn_mm[(c2, ht)])
                    if ht > 0:
                        scalar.wait_ge(semT, dn_ssmm[(c2, ht - 1)])
                    scalar.activation(ysq[:], ps_dn[:, psl], AF.Square)
                    scalar.last.then_inc(semS, 1)
                    cS.inc()
                dn_ysqs = cS.v

                @block.vector
                def _(vector, c2=c2, ht=ht, psl=psl):
                    vector.wait_ge(semT, dn_mm[(c2, ht)])
                    vector.tensor_copy(y_store[:, ht * 512:(ht + 1) * 512], ps_dn[:, psl])
                    vector.last.then_inc(semV, 1)
                    cV.inc()
                dn_ysq[(c2, ht)] = cV.v
                dn_ev[(c2, ht)] = dn_ysq[(c2, ht)]

                @block.tensor
                def _(tensor, c2=c2, ht=ht, dn_ysqs=dn_ysqs):
                    tensor.wait_ge(semS, dn_ysqs)
                    tensor.wait_ge(semV, dn_ysq[(c2, ht)])
                    tensor.matmul(ps_ss2[:, :], ones[:, :], ysq[:],
                                  start=(ht == 0), stop=(ht == 15))
                    tensor.last.then_inc(semT, 1)
                    cT.inc()
                dn_ssmm[(c2, ht)] = cT.v

            @block.scalar
            def _(scalar, c2=c2):
                scalar.wait_ge(semT, dn_ssmm[(c2, 15)])
                scalar.activation(srms2[:], ps_ss2[:, :], AF.Sqrt, bias=c_feps, scale=1.0 / D)
                scalar.last.then_inc(semS, 1)
                cS.inc()
            rms_done = cS.v

            @block.vector
            def _(vector, c2=c2, rms_done=rms_done):
                vector.wait_ge(semS, rms_done)
                vector.reciprocal(rb2[:], srms2[:])
                vector.drain()
                for ht in range(16):
                    sl = slice(ht * 512, (ht + 1) * 512)
                    vector.tensor_tensor(y_store[:, sl], y_store[:, sl], rb2[:], ALU.mult)
                vector.last.then_inc(semV, 1)
                cV.inc()
            ynorm_done = cV.v

            @block.scalar
            def _(scalar, c2=c2, ynorm_done=ynorm_done):
                scalar.wait_ge(semV, ynorm_done)
                for ht in range(16):
                    sl = slice(ht * 512, (ht + 1) * 512)
                    scalar.activation(y_store[:, sl], y_store[:, sl], AF.Silu)
                scalar.last.then_inc(semS, 1)
                cS.inc()
            g_done = cS.v

            for ht in range(16):
                pb = ht % 2

                @block.sync
                def _(sync, c2=c2, ht=ht):
                    prevk = (c2, ht - 4) if ht >= 4 else ((c2 - 1, 12 + ht) if c2 > 0 else None)
                    if prevk is not None and prevk in dn_comb:
                        sync.wait_ge(semV, dn_comb[prevk])
                    sync.dma_start(out=sxt[ht % 4][:],
                                   in_=xT[128 * ht:128 * (ht + 1),
                                          HALO + 512 * c2: HALO + 512 * c2 + 512]
                                   ).then_inc(semL, 16)
                    cL.inc(16)
                    sync.dma_start(out=ssT[ht % 4][:],
                                   in_=sT_dram[128 * ht:128 * (ht + 1), 512 * c2:512 * (c2 + 1)]
                                   ).then_inc(semL, 16)
                    cL.inc(16)
                dn_xs_load[(c2, ht)] = cL.v

                @block.vector
                def _(vector, c2=c2, ht=ht, pb=pb, g_done=g_done):
                    vector.wait_ge(semS, g_done)
                    vector.wait_ge(semL, dn_xs_load[(c2, ht)])
                    prevk = (c2, ht - 2) if ht >= 2 else ((c2 - 1, 14 + ht) if c2 > 0 else None)
                    if prevk is not None and prevk in dn_ost:
                        vector.wait_ge(semO, dn_ost[prevk])
                    sl = slice(ht * 512, (ht + 1) * 512)
                    vector.tensor_tensor(comb[:], ssT[ht % 4][:], y_store[:, sl], ALU.mult)
                    vector.drain()
                    vector.tensor_tensor(outb[pb][:], sxt[ht % 4][:], comb[:], ALU.add)
                    vector.last.then_inc(semV, 1)
                    cV.inc()
                dn_comb[(c2, ht)] = cV.v

                @block.sync
                def _(sync, c2=c2, ht=ht, pb=pb):
                    sync.wait_ge(semV, dn_comb[(c2, ht)])
                    sync.dma_start(out=o_out[128 * ht:128 * (ht + 1), 512 * c2:512 * (c2 + 1)],
                                   in_=outb[pb][:]).then_inc(semO, 16)
                    cO.inc(16)
                dn_ost[(c2, ht)] = cO.v

        @block.sync
        def _(sync):
            sync.wait_ge(semO, cO.v)

        nc.leave_named_scope("P7_down", _sid_P7_down, False)
        dn.close()
        gsc.close()

    return nc


def kernel(L0_attn_x, L0_attn_w_qkv, L0_attn_w_out, L0_attn_q_norm_w, L0_attn_k_norm_w,
           L0_ffn_gate_up_packed, L0_ffn_gate_up_scales, L0_ffn_down_packed,
           L0_ffn_down_scales):
    SD, fD = _perm_d()
    SI, fI = _perm_i()

    x = np.asarray(L0_attn_x, dtype=np.float32)
    wqkv = np.asarray(L0_attn_w_qkv, dtype=np.float32)
    wout = np.asarray(L0_attn_w_out, dtype=np.float32)
    qnw = np.asarray(L0_attn_q_norm_w, dtype=np.float32)
    knw = np.asarray(L0_attn_k_norm_w, dtype=np.float32)
    gup = np.asarray(L0_ffn_gate_up_packed)
    gus = np.asarray(L0_ffn_gate_up_scales)
    dnp_ = np.asarray(L0_ffn_down_packed)
    dns = np.asarray(L0_ffn_down_scales)

    wqkvT_st = np.ascontiguousarray(wqkv.T[SD, :])
    woutT_st = np.ascontiguousarray(wout.T[:, SD])
    wqpp_st = np.ascontiguousarray(np.tile(qnw, 2)[:, None])
    wkpp_st = np.ascontiguousarray(np.tile(knw, 2)[:, None])

    cols = np.concatenate([SI, II + SI])
    pk16_st = np.ascontiguousarray(gup.view('<u2').reshape(2 * II, 512)[cols, :].T)
    screp_st = np.ascontiguousarray(np.repeat(gus.astype(np.float16), 32, axis=0)[:, cols])
    dpk16_st = np.ascontiguousarray(dnp_.view('<u2').reshape(D, 1408)[SD, :].T)
    dscrep_st = np.ascontiguousarray(np.repeat(dns.astype(np.float16), 32, axis=0)[:, SD])

    eye_st = np.eye(128, dtype=np.float32).astype(bfdt)
    bd_st = np.zeros((128, 128), np.float32)
    bd_st[:64, :64] = 1.0
    bd_st[64:, 64:] = 1.0
    bd_st = bd_st.astype(bfdt)
    ones_st = np.ones((128, 128), np.float32).astype(bfdt)
    consts_st = np.zeros((128, 8), np.float32)
    consts_st[:, 0] = 0.5
    consts_st[:, 1] = 64.0 * 1e-6
    consts_st[:, 2] = 1e-6
    consts_st[:, 3] = 1e-5
    consts_st[:, 4] = 1e-6

    i_ = np.arange(128)[:, None]
    jj = np.arange(640)[None, :]
    win = (jj > i_) & (jj <= i_ + 512)
    mask_by_half = {}
    for half in (0, 1):
        m = np.zeros((8, 128, 640), np.float32)
        for t in range(8):
            ok = win.copy()
            if half == 0:
                ok = ok & ((128 * t + jj) >= 512)
            m[t] = np.where(ok, 0.0, NEG)
        mask_by_half[half] = np.ascontiguousarray(
            m.transpose(1, 0, 2).reshape(128, 8 * 640)).astype(bfdt)

    invf = 1.0 / (10000.0 ** (np.arange(0, HD, 2, dtype=np.float64) / HD))
    rope_by_half = {}
    for half in (0, 1):
        q0 = SOWN * half
        pos = np.maximum(q0 - HALO + np.arange(SK, dtype=np.float64), 0.0)
        ang = pos[None, :] * invf[:, None]
        ang2 = np.concatenate([ang, ang], axis=0)
        cos = np.concatenate([np.cos(ang2)] * 2, axis=0).astype(np.float32)
        sin = np.concatenate([np.sin(ang2)] * 2, axis=0).astype(np.float32)
        rope_by_half[half] = (np.ascontiguousarray(cos), np.ascontiguousarray(sin))

    in_maps = []
    for c in range(8):
        b, half = c // 2, c % 2
        q0 = SOWN * half
        xb = x[b].T[SD, :]
        xs = np.zeros((D, SK), np.float32)
        lo = q0 - HALO
        src_lo = max(lo, 0)
        xs[:, src_lo - lo:] = xb[:, src_lo:q0 + SOWN]
        cos_st, sin_st = rope_by_half[half]
        in_maps.append(dict(
            xT=np.ascontiguousarray(xs), wqkvT=wqkvT_st, woutT=woutT_st,
            wqpp=wqpp_st, wkpp=wkpp_st, pk16=pk16_st, screp=screp_st,
            dpk16=dpk16_st, dscrep=dscrep_st, masks_in=mask_by_half[half],
            cos_in=cos_st, sin_in=sin_st, eye_in=eye_st, bd_in=bd_st,
            ones_in=ones_st, consts_in=consts_st,
        ))

    nc = build_nc()
    res = run_bass_kernel_spmd(nc, in_maps, core_ids=list(range(8)))

    out = np.empty((4, 2048, D), np.float32)
    for c in range(8):
        b, half = c // 2, c % 2
        o = res.results[c]["o_out"]
        out[b, SOWN * half:SOWN * (half + 1), :] = o[fD, :].T
    if DEBUG_TAPS:
        kernel.last_results = res
    return out


# version 29
# speedup vs baseline: 1.1484x; 1.1484x over previous
import contextlib
import numpy as np
import ml_dtypes
import concourse.bass as bass
import concourse.mybir as mybir
from concourse.bass_utils import run_bass_kernel_spmd

F32 = mybir.dt.float32
BF16 = mybir.dt.bfloat16
F16 = mybir.dt.float16
U16 = mybir.dt.uint16
AF = mybir.ActivationFunctionType
ALU = mybir.AluOpType
bfdt = ml_dtypes.bfloat16

D = 2048
NQH, NKV, HD = 32, 8, 64
SK = 1536
SOWN = 1024
HALO = 512
II = 5632
NEG = -1e30

DEBUG_TAPS = False
import os
PHASES = int(os.environ.get("KPH", "9"))


class _StopEmit(Exception):
    pass


_SKIP_TRACK = {"wait_ge", "nop", "drain", "maybe_wait_ge", "sem_inc", "wait_op"}


class _T:

    def __init__(self, eng):
        self._e = eng
        self.last = None

    def __getattr__(self, k):
        f = getattr(self._e, k)
        if callable(f) and not k.startswith("_"):
            def g(*a, _f=f, _k=k, **kw):
                r = _f(*a, **kw)
                if _k not in _SKIP_TRACK and r is not None:
                    self.last = r
                return r
            return g
        return f


def _wrap_block(block):
    class W:
        def __getattr__(self, name):
            dec = getattr(block, name)

            def mydec(fn):
                def inner(eng):
                    return fn(_T(eng))
                inner.__name__ = getattr(fn, "__name__", "_")
                return dec(inner)
            return mydec
    return W()


def _perm_d():
    h = np.arange(D)
    f = (h % 4) * 512 + (h // 8) * 2 + ((h // 4) % 2)
    s = np.empty(D, np.int64)
    s[f] = h
    return s, f


def _perm_i():
    i = np.arange(II)
    f = (i % 4) * 1408 + (i // 8) * 2 + ((i // 4) % 2)
    s = np.empty(II, np.int64)
    s[f] = i
    return s, f


class Ctr:
    def __init__(self):
        self.v = 0

    def inc(self, n=1):
        self.v += n
        return self.v


def _m3(ap, a):
    return ap.rearrange("(a p) c -> p a c", p=128)


def _s3(ap, a):
    return ap.rearrange("p (a c) -> p a c", a=a)


def build_nc():
    try:
        return _build_nc_inner()
    except _StopEmit as e:
        return e.nc


def _build_nc_inner():
    nc = bass.Bass("TRN2", target_bir_lowering=False, debug=False)

    xT = nc.dram_tensor("xT", [D, SK], F32, kind="ExternalInput").ap()
    wqkvT = nc.dram_tensor("wqkvT", [D, 3072], F32, kind="ExternalInput").ap()
    woutT = nc.dram_tensor("woutT", [D, D], F32, kind="ExternalInput").ap()
    wqpp = nc.dram_tensor("wqpp", [128, 1], F32, kind="ExternalInput").ap()
    wkpp = nc.dram_tensor("wkpp", [128, 1], F32, kind="ExternalInput").ap()
    pk16 = nc.dram_tensor("pk16", [512, 11264], U16, kind="ExternalInput").ap()
    screp = nc.dram_tensor("screp", [512, 11264], F16, kind="ExternalInput").ap()
    dpk16 = nc.dram_tensor("dpk16", [1408, D], U16, kind="ExternalInput").ap()
    dscrep = nc.dram_tensor("dscrep", [1408, D], F16, kind="ExternalInput").ap()
    masks_in = nc.dram_tensor("masks_in", [128, 8 * 640], BF16, kind="ExternalInput").ap()
    cos_in = nc.dram_tensor("cos_in", [128, SK], F32, kind="ExternalInput").ap()
    sin_in = nc.dram_tensor("sin_in", [128, SK], F32, kind="ExternalInput").ap()
    eye_in = nc.dram_tensor("eye_in", [128, 128], BF16, kind="ExternalInput").ap()
    bd_in = nc.dram_tensor("bd_in", [128, 128], BF16, kind="ExternalInput").ap()
    ones_in = nc.dram_tensor("ones_in", [128, 128], BF16, kind="ExternalInput").ap()
    consts_in = nc.dram_tensor("consts_in", [128, 8], F32, kind="ExternalInput").ap()

    o_out = nc.dram_tensor("o_out", [D, SOWN], F32, kind="ExternalOutput").ap()
    sk_ = "ExternalOutput" if DEBUG_TAPS else "Internal"
    sT_dram = nc.dram_tensor("sT_dram", [D, SOWN], F32, kind=sk_).ap()
    wdn_dram = nc.dram_tensor("wdn_dram", [II, D], BF16, kind=sk_).ap()
    if DEBUG_TAPS:
        xn_dram = nc.dram_tensor("xn_dram", [D, SK], BF16, kind="ExternalOutput").ap()
        at_dram = nc.dram_tensor("at_dram", [D, SOWN], BF16, kind="ExternalOutput").ap()
        h_dram = nc.dram_tensor("h_dram", [II, SOWN], BF16, kind="ExternalOutput").ap()

    es = contextlib.ExitStack()
    _stacks = []
    with es:
        E = es.enter_context

        semL = E(nc.semaphore("semL"))
        semG = E(nc.semaphore("semG"))
        semV = E(nc.semaphore("semV"))
        semS = E(nc.semaphore("semS"))
        semT = E(nc.semaphore("semT"))
        semB = E(nc.semaphore("semB"))
        semO = E(nc.semaphore("semO"))
        cL, cG, cV, cS, cT, cB, cO = (Ctr() for _ in range(7))

        block = _wrap_block(E(nc.Block()))

        def barrier():
            tgt = cB.inc(5)
            for name in ("sync", "gpsimd", "vector", "scalar", "tensor"):
                @getattr(block, name)
                def _(eng, name=name):
                    if name != "sync":
                        eng.drain()
                    eng.nop().then_inc(semB, 1)
                    eng.wait_ge(semB, tgt)


        def maybe_stop(n):
            if PHASES <= n:
                @block.sync
                def _(sync):
                    sync.wait_ge(semO, cO.v)
                for s in reversed(_stacks):
                    try:
                        s.close()
                    except Exception:
                        pass
                err = _StopEmit()
                err.nc = nc
                raise err

        consts = E(nc.sbuf_tensor("consts", [128, 8], F32))
        eye = E(nc.sbuf_tensor("eye", [128, 128], BF16))
        bd64 = E(nc.sbuf_tensor("bd64", [128, 128], BF16))
        ones = E(nc.sbuf_tensor("ones", [128, 128], BF16))
        swq = E(nc.sbuf_tensor("swq", [128, 1], F32))
        swk = E(nc.sbuf_tensor("swk", [128, 1], F32))

        xn_stack = contextlib.ExitStack()
        _stacks.append(xn_stack)
        xnT = xn_stack.enter_context(nc.sbuf_tensor("xnT", [128, 16 * SOWN], BF16, side="right"))

        atn = contextlib.ExitStack()
        _stacks.append(atn)

        att = contextlib.ExitStack()
        _stacks.append(att)
        EA = att.enter_context
        masks = EA(nc.sbuf_tensor("masks", [128, 8 * 640], BF16))
        qhat = EA(nc.sbuf_tensor("qhat", [128, 16 * SOWN], BF16))
        khat = EA(nc.sbuf_tensor("khat", [128, 8 * SK], BF16))
        vst = EA(nc.sbuf_tensor("vst", [128, 12 * 512], BF16))
        rope_sc = contextlib.ExitStack()
        _stacks.append(rope_sc)
        ER = rope_sc.enter_context
        cosT = ER(nc.sbuf_tensor("cosT", [128, SK], F32))
        sinT = ER(nc.sbuf_tensor("sinT", [128, SK], F32))
        xnH = ER(nc.sbuf_tensor("xnH", [128, 16 * HALO], BF16))

        @block.sync
        def _(sync):
            for dst, src in ((consts[:], consts_in), (eye[:], eye_in),
                             (bd64[:], bd_in), (ones[:], ones_in),
                             (swq[:], wqpp), (swk[:], wkpp),
                             (masks[:], masks_in), (cosT[:], cos_in),
                             (sinT[:], sin_in)):
                sync.dma_start(out=dst, in_=src).then_inc(semL, 16)
                cL.inc(16)

        if os.environ.get("KBAR"):
            barrier()
        maybe_stop(0)

        c_half = consts[:, 0:1]
        c_qeps = consts[:, 1:2]
        c_keps = consts[:, 2:3]
        c_lneps = consts[:, 3:4]
        c_feps = consts[:, 4:5]

        _sid_P1_ln = nc.enter_named_scope("P1_ln", False)[0]
        ln = contextlib.ExitStack()
        _stacks.append(ln)
        EL = ln.enter_context
        xin = [EL(nc.sbuf_tensor(f"xin{p}", [128, 16 * 256], F32, side="right")) for p in range(2)]
        xbb_ = EL(nc.sbuf_tensor("xbb", [128, 16 * 256], BF16, side="right"))
        xsq_ = EL(nc.sbuf_tensor("xsq", [128, 16 * 256], BF16, side="right"))
        xbb = [xbb_, xbb_]
        xsq = [xsq_, xsq_]
        xm2 = [EL(nc.sbuf_tensor(f"xm2{q}", [128, 256], F32, side="right")) for q in range(4)]
        mtile = EL(nc.sbuf_tensor("mtile", [128, 256], F32, side="right"))
        m2tile = EL(nc.sbuf_tensor("m2tile", [128, 256], F32, side="right"))
        vtile = EL(nc.sbuf_tensor("vtile", [128, 256], F32, side="right"))
        stile = EL(nc.sbuf_tensor("stile", [128, 256], F32, side="right"))
        rtile = EL(nc.sbuf_tensor("rtile", [128, 256], F32, side="right"))
        ps_mb = EL(nc.psum_tensor("ps_mb", [128, 1024], F32))
        ps_sq = EL(nc.psum_tensor("ps_sq", [128, 1024], F32))

        ln_load_done, ln_ev_done, ln_sq_done, ln_mm_done = [], [], [], []
        ln_var_done, ln_std_done = [], []

        KLNC = int(os.environ.get("KLNC", "6"))
        KLNS = int(os.environ.get("KLNS", "9"))
        for c in range(KLNC):
            p = c % 2

            @block.sync
            def _(sync, c=c, p=p):
                if c >= 2:
                    sync.wait_ge(semV, ln_ev_done[c - 2])
                sync.dma_start(out=_s3(xin[p][:], 16),
                               in_=_m3(xT[:, 256 * c:256 * (c + 1)], 16)).then_inc(semL, 16)
                cL.inc(16)
            ln_load_done.append(cL.v)

            @block.vector
            def _(vector, c=c, p=p):
                if KLNS < 2:
                    return
                vector.wait_ge(semL, ln_load_done[c])
                if c >= 1:
                    vector.wait_ge(semT, ln_mm_done[c - 1])
                for rt in range(16):
                    sl = slice(rt * 256, (rt + 1) * 256)
                    vector.tensor_copy(xbb[p][:, sl], xin[p][:, sl])
                for rt in range(16):
                    sl = slice(rt * 256, (rt + 1) * 256)
                    vector.tensor_tensor(xsq[p][:, sl], xbb[p][:, sl], xbb[p][:, sl], ALU.mult)
                vector.last.then_inc(semV, 1)
                cV.inc()
            ln_sq_done.append(cV.v)

            @block.tensor
            def _(tensor, c=c, p=p):
                if KLNS < 3:
                    return
                tensor.wait_ge(semV, ln_sq_done[c])
                if c >= 2:
                    tensor.wait_ge(semV, ln_var_done[c - 2])
                psl = slice(p * 512, p * 512 + 256)
                for rt in range(16):
                    sl = slice(rt * 256, (rt + 1) * 256)
                    tensor.matmul(ps_mb[:, psl], ones[:, :], xbb[p][:, sl],
                                  start=(rt == 0), stop=(rt == 15))
                for rt in range(16):
                    sl = slice(rt * 256, (rt + 1) * 256)
                    tensor.matmul(ps_sq[:, psl], ones[:, :], xsq[p][:, sl],
                                  start=(rt == 0), stop=(rt == 15))
                tensor.last.then_inc(semT, 1)
                cT.inc()
            ln_mm_done.append(cT.v)

            @block.vector
            def _(vector, c=c, p=p):
                if KLNS < 4:
                    return
                vector.wait_ge(semT, ln_mm_done[c])
                psl = slice(p * 512, p * 512 + 256)
                vector.tensor_scalar(mtile[:], ps_mb[:, psl], 1.0 / D, None, ALU.mult)
                vector.drain()
                vector.tensor_tensor(m2tile[:], mtile[:], mtile[:], ALU.mult)
                vector.drain()
                vector.scalar_tensor_tensor(vtile[:], ps_sq[:, psl], 1.0 / D, m2tile[:],
                                            ALU.mult, ALU.subtract)
                vector.last.then_inc(semV, 1)
                cV.inc()
            ln_var_done.append(cV.v)

            @block.scalar
            def _(scalar, c=c):
                if KLNS < 5:
                    return
                scalar.wait_ge(semV, ln_var_done[c])
                scalar.activation(stile[:], vtile[:], AF.Sqrt, bias=c_lneps, scale=1.0)
                scalar.last.then_inc(semS, 1)
                cS.inc()
            ln_std_done.append(cS.v)

            @block.vector
            def _(vector, c=c, p=p):
                if KLNS < 6:
                    return
                vector.wait_ge(semS, ln_std_done[c])
                vector.reciprocal(rtile[:], stile[:])
                vector.drain()
                def _dst(rt):
                    if c < 2:
                        return xnH[:, rt * HALO + 256 * c: rt * HALO + 256 * (c + 1)]
                    j0 = 256 * c - 512
                    return xnT[:, rt * SOWN + j0: rt * SOWN + j0 + 256]
                for rt0 in range(0, 16, 4):
                    for q in range(4):
                        sl = slice((rt0 + q) * 256, (rt0 + q + 1) * 256)
                        vector.tensor_tensor(xm2[q][:], xin[p][:, sl], mtile[:], ALU.subtract)
                    for q in range(4):
                        vector.tensor_tensor(_dst(rt0 + q), xm2[q][:], rtile[:], ALU.mult)
                vector.last.then_inc(semV, 1)
                cV.inc()
            ln_ev_done.append(cV.v)

        barrier()
        if DEBUG_TAPS:
            @block.sync
            def _(sync):
                for rt in range(16):
                    sync.dma_start(out=xn_dram[128 * rt:128 * (rt + 1), 0:HALO],
                                   in_=xnH[:, rt * HALO:(rt + 1) * HALO]).then_inc(semO, 16)
                    cO.inc(16)
                    sync.dma_start(out=xn_dram[128 * rt:128 * (rt + 1), HALO:SK],
                                   in_=xnT[:, rt * SOWN:(rt + 1) * SOWN]).then_inc(semO, 16)
                    cO.inc(16)
        ln.close()
        maybe_stop(1)


        nc.leave_named_scope("P1_ln", _sid_P1_ln, False)
        _sid_P2_qkv = nc.enter_named_scope("P2_qkv", False)[0]
        qk = contextlib.ExitStack()
        _stacks.append(qk)
        EQ = qk.enter_context
        wcol = [EQ(nc.sbuf_tensor(f"wcol{p}", [128, 16 * 128], BF16, side="right")) for p in range(2)]
        sqscr2 = [EQ(nc.sbuf_tensor(f"sqscr{p}", [128, 512], BF16, side="right")) for p in range(2)]
        sqs2 = [EQ(nc.sbuf_tensor(f"sqs{p}", [128, 512], F32, side="right")) for p in range(2)]
        rqs = EQ(nc.sbuf_tensor("rqs", [128, 512], F32, side="right"))
        qns = EQ(nc.sbuf_tensor("qns", [128, 512], F32, side="right"))
        rots = EQ(nc.sbuf_tensor("rots", [128, 512], F32, side="right"))
        t1s = EQ(nc.sbuf_tensor("t1s", [128, 512], F32, side="right"))
        t2s = EQ(nc.sbuf_tensor("t2s", [128, 512], F32, side="right"))
        vtmps = EQ(nc.sbuf_tensor("vtmps", [128, 512], BF16, side="right"))
        ps_qkv = EQ(nc.psum_tensor("ps_qkv", [128, 1024], F32))
        ps_ss = EQ(nc.psum_tensor("ps_ss", [128, 1024], F32))
        ps_vt = EQ(nc.psum_tensor("ps_vt", [128, 1024], BF16))

        dd = contextlib.ExitStack()
        _stacks.append(dd)
        ED = dd.enter_context
        HC = 1024
        dpkb = [ED(nc.sbuf_tensor(f"dpkb{p}", [128, HC], U16, side="right")) for p in range(2)]
        dscb = [ED(nc.sbuf_tensor(f"dscb{p}", [128, HC], F16, side="right")) for p in range(2)]
        ddv = [ED(nc.sbuf_tensor(f"ddv{k}", [128, HC], U16, side="right")) for k in range(4)]
        dds = [ED(nc.sbuf_tensor(f"dds{k}", [128, HC], U16, side="right")) for k in range(4)]
        dscq = ED(nc.sbuf_tensor("dscq", [128, HC], F16, side="right"))
        ddec = [[ED(nc.sbuf_tensor(f"ddec{p}_{k}", [128, HC], BF16, side="right")) for k in range(4)]
                for p in range(2)]
        dd_load, dd_dec, dd_st = ({} for _ in range(3))

        def emit_dd_load(hc):
            rt, hf = hc // 2, hc % 2
            pb = hc % 2

            @block.sync
            def _(sync, rt=rt, hf=hf, pb=pb, hc=hc):
                if hc >= 2:
                    sync.wait_ge(semV, dd_dec[hc - 2])
                sync.dma_start(out=dpkb[pb][:],
                               in_=dpk16[128 * rt:128 * (rt + 1), HC * hf:HC * (hf + 1)]
                               ).then_inc(semL, 16)
                cL.inc(16)
                sync.dma_start(out=dscb[pb][:],
                               in_=dscrep[128 * rt:128 * (rt + 1), HC * hf:HC * (hf + 1)]
                               ).then_inc(semL, 16)
                cL.inc(16)
            dd_load[hc] = cL.v

        def emit_dd_decode(hc):
            rt, hf = hc // 2, hc % 2
            pb = hc % 2

            @block.vector
            def _(vector, rt=rt, hf=hf, pb=pb, hc=hc):
                vector.wait_ge(semL, dd_load[hc])
                if hc >= 2:
                    vector.wait_ge(semO, dd_st[hc - 2])
                W = dpkb[pb][:]
                vector.tensor_scalar(dscq[:], dscb[pb][:], 16384.0, None, ALU.mult)
                for kk in range(4):
                    sh = 9 - 4 * kk
                    if sh >= 0:
                        vector.tensor_scalar(ddv[kk][:], W, sh, 7 << 9,
                                             ALU.logical_shift_left, ALU.bitwise_and)
                    else:
                        vector.tensor_scalar(ddv[kk][:], W, -sh, 7 << 9,
                                             ALU.logical_shift_right, ALU.bitwise_and)
                for kk in range(4):
                    vector.tensor_scalar(dds[kk][:], W, 4 * kk + 3, 15,
                                         ALU.logical_shift_right, ALU.logical_shift_left)
                for kk in range(4):
                    vector.tensor_tensor(ddv[kk][:], ddv[kk][:], dds[kk][:], ALU.add)
                for kk in range(4):
                    vector.tensor_tensor(ddec[pb][kk][:], ddv[kk][:].bitcast(F16),
                                         dscq[:], ALU.mult)
                vector.last.then_inc(semV, 1)
                cV.inc()
            dd_dec[hc] = cV.v

            @block.sync
            def _(sync, rt=rt, hf=hf, pb=pb, hc=hc):
                sync.wait_ge(semV, dd_dec[hc])
                for kk in range(4):
                    it = 11 * kk + rt
                    sync.dma_start(out=wdn_dram[128 * it:128 * (it + 1), HC * hf:HC * (hf + 1)],
                                   in_=ddec[pb][kk][:]).then_inc(semO, 16)
                    cO.inc(16)
            dd_st[hc] = cO.v

        def rhs_xn(kt, ch):
            if ch == 0:
                return xnH[:, kt * HALO: kt * HALO + 512]
            j0 = 512 * (ch - 1)
            return xnT[:, kt * SOWN + j0: kt * SOWN + j0 + 512]

        qk_wload = {}
        qk_mm = {}
        qk_ev = {}
        qk_ssmm = {}
        qk_sqrt = {}
        qk_vtp = {}
        qk_vtmp = {}

        seq2 = []
        for ot in range(24):
            for ch in ([0, 1, 2] if ot >= 16 else [1, 2]):
                seq2.append((ot, ch))

        prev_on_parity = {}
        prev_ev = None
        prev_v = None
        qk_pos = [p for p in seq2 if p[0] < 20]
        NQK = len(qk_pos)
        sc_order = [("sq", 0)]
        for k in range(1, NQK):
            sc_order += [("sq", k), ("sqrt", k - 1)]
        sc_order.append(("sqrt", NQK - 1))
        cS_qk0 = cS.v
        qk_sq_s = {}
        for idx, (kind, k) in enumerate(sc_order):
            if kind == "sq":
                qk_sq_s[qk_pos[k]] = cS_qk0 + idx + 1
            else:
                qk_sqrt[qk_pos[k]] = cS_qk0 + idx + 1
        for ot in range(24):
            if ot >= 2:
                emit_dd_decode(ot - 2)
            if ot < 22:
                emit_dd_load(ot)

            @block.gpsimd
            def _(gpsimd, ot=ot):
                if ot >= 2:
                    gpsimd.wait_ge(semT, qk_mm[(ot - 2, 2)])
                gpsimd.dma_start(
                    out=_s3(wcol[ot % 2][:], 16),
                    in_=_m3(wqkvT[:, 128 * ot:128 * (ot + 1)], 16)).then_inc(semG, 16)
                cG.inc(16)
            qk_wload[ot] = cG.v

            for ch in ([0, 1, 2] if ot >= 16 else [1, 2]):
                pos = seq2.index((ot, ch))
                par = pos % 2
                psl = slice(par * 512, (par + 1) * 512)

                @block.tensor
                def _(tensor, ot=ot, ch=ch, par=par, psl=psl):
                    tensor.wait_ge(semG, qk_wload[ot])
                    if par in prev_on_parity:
                        tensor.wait_ge(semV, qk_ev[prev_on_parity[par]])
                    for kt in range(16):
                        tensor.matmul(ps_qkv[:, psl], wcol[ot % 2][:, kt * 128:(kt + 1) * 128],
                                      rhs_xn(kt, ch), start=(kt == 0), stop=(kt == 15))
                    tensor.last.then_inc(semT, 1)
                    cT.inc()
                qk_mm[(ot, ch)] = cT.v
                prev_on_parity[par] = (ot, ch)

                if ot < 20:
                    is_q = ot < 16
                    kq = qk_pos.index((ot, ch))
                    sssl = slice((kq % 2) * 512, (kq % 2) * 512 + 512)

                    @block.tensor
                    def _(tensor, ot=ot, ch=ch, kq=kq, sssl=sssl):
                        tensor.wait_ge(semS, qk_sq_s[(ot, ch)])
                        if kq >= 2:
                            tensor.wait_ge(semS, qk_sqrt[qk_pos[kq - 2]])
                        tensor.matmul(ps_ss[:, sssl], bd64[:, :], sqscr2[kq % 2][:],
                                      start=True, stop=True)
                        tensor.last.then_inc(semT, 1)
                        cT.inc()
                    qk_ssmm[(ot, ch)] = cT.v

                    @block.vector
                    def _(vector, ot=ot, ch=ch, psl=psl, is_q=is_q, kq=kq):
                        vector.wait_ge(semS, qk_sqrt[(ot, ch)])
                        vector.reciprocal(rqs[:], sqs2[kq % 2][:])
                        vector.drain()
                        vector.scalar_tensor_tensor(qns[:], ps_qkv[:, psl],
                                                    (swq[:, 0:1] if is_q else swk[:, 0:1]),
                                                    rqs[:], ALU.mult, ALU.mult)
                        vector.drain()
                        for hb in range(2):
                            b0 = hb * 64
                            vector.tensor_scalar(rots[b0:b0 + 32, :], qns[b0 + 32:b0 + 64, :],
                                                 -1.0, None, ALU.mult)
                            vector.tensor_copy(rots[b0 + 32:b0 + 64, :], qns[b0:b0 + 32, :])
                        j0 = 512 * ch
                        cs = cosT[:, j0:j0 + 512]
                        sn = sinT[:, j0:j0 + 512]
                        vector.tensor_tensor(t1s[:], qns[:], cs, ALU.mult)
                        vector.tensor_tensor(t2s[:], rots[:], sn, ALU.mult)
                        vector.drain()
                        if is_q:
                            jq = 512 * (ch - 1)
                            dst = qhat[:, ot * SOWN + jq: ot * SOWN + jq + 512]
                            vector.tensor_tensor(dst, t1s[:], t2s[:], ALU.add)
                        else:
                            gA = 2 * (ot - 16)
                            jk = 512 * ch
                            for (gg, pr) in ((gA, slice(0, 64)), (gA + 1, slice(64, 128))):
                                for half in (slice(0, 64), slice(64, 128)):
                                    vector.tensor_tensor(
                                        khat[half, gg * SK + jk: gg * SK + jk + 512],
                                        t1s[pr, :], t2s[pr, :], ALU.add)
                        vector.last.then_inc(semV, 1)
                        cV.inc()
                    qk_ev[(ot, ch)] = cV.v
                    prev_ev = cV.v
                else:
                    @block.vector
                    def _(vector, ot=ot, ch=ch, psl=psl, prev_v=prev_v):
                        vector.wait_ge(semT, qk_mm[(ot, ch)])
                        if prev_v is not None:
                            vector.wait_ge(semT, qk_vtp[prev_v])
                        vector.tensor_copy(vtmps[:], ps_qkv[:, psl])
                        vector.last.then_inc(semV, 1)
                        cV.inc()
                    qk_vtmp[(ot, ch)] = cV.v

                    @block.tensor
                    def _(tensor, ot=ot, ch=ch):
                        tensor.wait_ge(semV, qk_vtmp[(ot, ch)])
                        for u in range(4):
                            tensor.transpose(ps_vt[:, u * 128:(u + 1) * 128],
                                             vtmps[:, u * 128:(u + 1) * 128], eye[:, :])
                        tensor.last.then_inc(semT, 1)
                        cT.inc()
                    qk_vtp[(ot, ch)] = cT.v
                    prev_v = (ot, ch)

                    @block.vector
                    def _(vector, ot=ot, ch=ch):
                        vector.wait_ge(semT, qk_vtp[(ot, ch)])
                        for u in range(4):
                            ktile = ch * 4 + u
                            dst = vst[:, ktile * 512 + 128 * (ot - 20):
                                      ktile * 512 + 128 * (ot - 20) + 128]
                            vector.tensor_copy(dst, ps_vt[:, u * 128:(u + 1) * 128])
                        vector.last.then_inc(semV, 1)
                        cV.inc()
                    qk_ev[(ot, ch)] = cV.v

        @block.scalar
        def _(scalar):
            for kind, k in sc_order:
                ot, ch = qk_pos[k]
                is_q = ot < 16
                par = seq2.index((ot, ch)) % 2
                psl = slice(par * 512, (par + 1) * 512)
                sssl = slice((k % 2) * 512, (k % 2) * 512 + 512)
                if kind == "sq":
                    scalar.wait_ge(semT, qk_mm[(ot, ch)])
                    aa = scalar.activation(sqscr2[k % 2][:], ps_qkv[:, psl], AF.Square)
                else:
                    scalar.wait_ge(semT, qk_ssmm[(ot, ch)])
                    if k >= 2:
                        scalar.wait_ge(semV, qk_ev[qk_pos[k - 2]])
                    aa = scalar.activation(sqs2[k % 2][:], ps_ss[:, sssl], AF.Sqrt,
                                           bias=(c_qeps if is_q else c_keps),
                                           scale=(1.0 if is_q else 1.0 / 64))
                aa.then_inc(semS, 1)
                cS.inc()
        assert cS.v == cS_qk0 + 2 * NQK

        emit_dd_decode(20)
        emit_dd_decode(21)
        barrier()
        dd.close()
        qk.close()
        rope_sc.close()
        attnT = atn.enter_context(nc.sbuf_tensor("attnT", [128, 16 * SOWN], BF16))
        maybe_stop(2)

        nc.leave_named_scope("P2_qkv", _sid_P2_qkv, False)
        _sid_P3_attn = nc.enter_named_scope("P3_attn", False)[0]
        at = contextlib.ExitStack()
        _stacks.append(at)
        ET = at.enter_context
        ptil = [[ET(nc.sbuf_tensor(f"ptil{q}_{p}", [128, 640], BF16, side="right")) for p in range(2)]
                for q in range(2)]
        ptn = [[ET(nc.sbuf_tensor(f"ptn{q}_{p}", [128, 640], BF16, side="right")) for p in range(2)]
               for q in range(2)]
        ptrs = [[ET(nc.sbuf_tensor(f"ptrs{q}_{p}", [128, 640], BF16, side="right")) for p in range(2)]
                for q in range(2)]
        ssc = [[ET(nc.sbuf_tensor(f"ssc{q}_{p}", [128, 640], F32, side="right")) for p in range(2)]
               for q in range(4)]
        nmx2 = ET(nc.sbuf_tensor("nmx2", [128, 8], F32, side="right"))
        zz2 = ET(nc.sbuf_tensor("zz2", [128, 8], F32, side="right"))
        rz2 = [ET(nc.sbuf_tensor(f"rz2_{p}", [128, 2], F32, side="right")) for p in range(4)]
        strail = ET(nc.sbuf_tensor("strail", [128, 2], F32, side="right"))
        ps_s = [ET(nc.psum_tensor(f"ps_s{p}", [128, 1024], F32)) for p in range(2)]
        ps_pt = [ET(nc.psum_tensor(f"ps_pt{p}", [128, 1024], BF16)) for p in range(2)]
        ps_pv = [ET(nc.psum_tensor(f"ps_pv{p}", [64, 512], F32)) for p in range(2)]

        pairs = []
        for t in range(8):
            for h0 in range(0, NQH, 2):
                pairs.append((t, h0))
        NP = len(pairs)
        at_sc, at_scev, at_mx, at_exp, at_tp, at_pv, at_pn = ({} for _ in range(7))
        at_ptev, at_evac, vw2 = ({} for _ in range(3))

        def em_scores(i):
            t, h0 = pairs[i]

            @block.tensor
            def _(tensor, t=t, heads=(h0, h0 + 1), i=i):
                if i >= 1:
                    tensor.wait_ge(semV, at_scev[i - 1])
                for j, h in enumerate(heads):
                    g = h >> 2
                    off = 64 * (h & 1)
                    lhs = qhat[off:off + 64,
                               (h >> 1) * SOWN + 128 * t: (h >> 1) * SOWN + 128 * t + 128]
                    k0 = g * SK + 128 * t
                    tensor.matmul(ps_s[j][:, 0:512], lhs,
                                  khat[off:off + 64, k0:k0 + 512], start=True, stop=True)
                    tensor.matmul(ps_s[j][:, 512:640], lhs,
                                  khat[off:off + 64, k0 + 512:k0 + 640], start=True, stop=True)
                tensor.last.then_inc(semT, 1)
                cT.inc()
            at_sc[i] = cT.v

        def em_v1(i):
            if not i < NP:
                return

            @block.vector
            def _(vector, i=i):
                vector.wait_ge(semT, at_sc[i])
                if i >= 4:
                    vector.wait_ge(semS, at_exp[i - 4])
                t = pairs[i][0]
                pp = i % 4
                ee = None
                for j in range(2):
                    ee = vector.scalar_tensor_tensor(
                        ssc[pp][j][:], ps_s[j][:, 0:640], 1.0,
                        masks[:, 640 * t:640 * t + 640], ALU.mult, ALU.add)
                ee.then_inc(semV, 1)
                cV.inc()
                at_scev[i] = cV.v

        def em_v2(i):
            has_evac = 3 <= i <= NP + 2
            has_max = i < NP
            has_ptev = 2 <= i <= NP + 1
            has_norm = 1 <= i <= NP
            if not (has_evac or has_max or has_ptev or has_norm):
                return

            @block.vector
            def _(vector, i=i, has_evac=has_evac, has_max=has_max,
                  has_ptev=has_ptev, has_norm=has_norm):
                if has_evac:
                    vector.wait_ge(semT, at_pv[i - 3])
                    pj = i - 3
                    t2, h02 = pairs[pj]
                    le = None
                    for j, h in enumerate((h02, h02 + 1)):
                        off = 64 * (h & 1)
                        dst = attnT[off:off + 64,
                                    (h >> 1) * SOWN + 128 * t2: (h >> 1) * SOWN + 128 * t2 + 128]
                        le = vector.tensor_copy(dst, ps_pv[j][:, 0:128])
                    le.then_inc(semV, 1)
                    cV.inc()
                    at_evac[pj] = cV.v
                if has_max:
                    if not has_evac:
                        vector.drain()
                    mm = None
                    pp = i % 4
                    for j in range(2):
                        jo = pp * 2 + j
                        mm = vector.reduce_max(nmx2[:, jo:jo + 1], ssc[pp][j][:],
                                               axis=mybir.AxisListType.X, negate=True)
                    mm.then_inc(semV, 1)
                    cV.inc()
                    at_mx[i] = cV.v
                if has_ptev:
                    vector.wait_ge(semT, at_tp[i - 2])
                    if i >= 3:
                        vector.wait_ge(semT, at_pv[i - 3])
                    pp = (i - 2) % 2
                    for j in range(2):
                        vector.tensor_copy(ptrs[pp][j][:], ps_pt[j][:, 0:640])
                    vector.drain()
                    vector.nop().then_inc(semV, 1)
                    cV.inc()
                    at_ptev[i - 2] = cV.v
                if has_norm:
                    vector.wait_ge(semS, at_exp[i - 1])
                    zo = ((i - 1) % 4) * 2
                    vector.reciprocal(rz2[((i - 1) % 4)][:], zz2[:, zo:zo + 2])
                    vector.drain()
                    vector.nop().then_inc(semV, 1)
                    cV.inc()
                    vw2[i] = cV.v

        EG = 8

        def em_exp_group(idxs):
            @block.scalar
            def _(scalar, idxs=tuple(idxs)):
                for i in idxs:
                    if 1 <= i <= NP:
                        scalar.wait_ge(semV, vw2[i])
                        pp = (i - 1) % 2
                        p4 = (i - 1) % 4
                        for j in range(2):
                            pn = scalar.activation(ptn[pp][j][:], ptil[pp][j][:],
                                                   AF.Copy, scale=rz2[p4][:, j:j + 1])
                        pn.then_inc(semS, 1)
                        cS.inc()
                        assert at_pn[i - 1] == cS.v
                    if i < NP:
                        scalar.wait_ge(semV, at_mx[i])
                        pp = i % 2
                        p4 = i % 4
                        for j in range(2):
                            jo = p4 * 2 + j
                            scalar.activation(ptil[pp][j][:], ssc[p4][j][:], AF.Exp,
                                              bias=nmx2[:, jo:jo + 1], scale=1.0,
                                              accum_out=zz2[:, jo:jo + 1])
                        tr = scalar.activation(strail[:], nmx2[:, 0:2], AF.Exp)
                        tr.then_inc(semS, 1)
                        cS.inc()
                        assert at_exp[i] == cS.v

        def em_tposes(pi):
            @block.tensor
            def _(tensor, pi=pi):
                tensor.wait_ge(semS, at_pn[pi])
                tensor.wait_ge(semV, vw2[pi + 1])
                pp = pi % 2
                for j in range(2):
                    for u in range(5):
                        tensor.transpose(ps_pt[j][:, u * 128:(u + 1) * 128],
                                         ptn[pp][j][:, u * 128:(u + 1) * 128], eye[:, :])
                tensor.last.then_inc(semT, 1)
                cT.inc()
            at_tp[pi] = cT.v

        def em_pv(pj):
            t, h0 = pairs[pj]

            @block.tensor
            def _(tensor, t=t, heads=(h0, h0 + 1), pj=pj):
                tensor.wait_ge(semV, at_ptev[pj])
                if pj >= 1:
                    tensor.wait_ge(semV, at_evac[pj - 1])
                pp = pj % 2
                for j, h in enumerate(heads):
                    g = h >> 2
                    for u in range(5):
                        ktile = t + u
                        tensor.matmul(ps_pv[j][:, 0:128],
                                      vst[:, ktile * 512 + 64 * g: ktile * 512 + 64 * g + 64],
                                      ptrs[pp][j][:, u * 128:(u + 1) * 128],
                                      start=(u == 0), stop=(u == 4))
                tensor.last.then_inc(semT, 1)
                cT.inc()
            at_pv[pj] = cT.v

        _run = cS.v
        for i in range(NP + 1):
            if 1 <= i <= NP:
                _run += 1
                at_pn[i - 1] = _run
            if i < NP:
                _run += 1
                at_exp[i] = _run
        cS_end_p3 = _run
        exp_pend = []
        for i in range(NP + 3):
            if i < NP:
                em_scores(i)
            if 2 <= i <= NP + 1:
                em_tposes(i - 2)
            em_v1(i)
            em_v2(i)
            if i <= NP:
                exp_pend.append(i)
                if len(exp_pend) == EG or i == NP:
                    em_exp_group(exp_pend)
                    exp_pend = []
            if 2 <= i <= NP + 1:
                em_pv(i - 2)
        assert cS.v == cS_end_p3, (cS.v, cS_end_p3)

        barrier()
        if DEBUG_TAPS:
            @block.sync
            def _(sync):
                for rt in range(16):
                    sync.dma_start(out=at_dram[128 * rt:128 * (rt + 1), :],
                                   in_=attnT[:, rt * SOWN:(rt + 1) * SOWN]).then_inc(semO, 16)
                    cO.inc(16)
        at.close()
        maybe_stop(3)

        nc.leave_named_scope("P3_attn", _sid_P3_attn, False)
        _sid_P4_wout = nc.enter_named_scope("P4_wout", False)[0]
        wo = contextlib.ExitStack()
        _stacks.append(wo)
        EW = wo.enter_context
        wocol = [EW(nc.sbuf_tensor(f"wocol{p}", [128, 16 * 128], BF16, side="right")) for p in range(2)]
        swo = [EW(nc.sbuf_tensor(f"swo{p}", [128, 512], F32, side="right")) for p in range(2)]
        ps_wo = EW(nc.psum_tensor("ps_wo", [128, 1024], F32))

        wo_load, wo_mm, wo_ev, wo_st = {}, {}, {}, {}
        for dt_ in range(16):
            @block.gpsimd
            def _(gpsimd, dt_=dt_):
                if dt_ >= 2:
                    gpsimd.wait_ge(semT, wo_mm[(dt_ - 2, 1)])
                gpsimd.dma_start(
                    out=_s3(wocol[dt_ % 2][:], 16),
                    in_=_m3(woutT[:, 128 * dt_:128 * (dt_ + 1)], 16)).then_inc(semG, 16)
                cG.inc(16)
            wo_load[dt_] = cG.v

            for c2 in range(2):
                par = c2
                psl = slice(par * 512, (par + 1) * 512)

                @block.tensor
                def _(tensor, dt_=dt_, c2=c2, psl=psl):
                    tensor.wait_ge(semG, wo_load[dt_])
                    prev = (dt_ - 1, c2)
                    if prev in wo_ev:
                        tensor.wait_ge(semV, wo_ev[prev])
                    for kt in range(16):
                        tensor.matmul(ps_wo[:, psl],
                                      wocol[dt_ % 2][:, kt * 128:(kt + 1) * 128],
                                      attnT[:, kt * SOWN + 512 * c2: kt * SOWN + 512 * c2 + 512],
                                      start=(kt == 0), stop=(kt == 15))
                    tensor.last.then_inc(semT, 1)
                    cT.inc()
                wo_mm[(dt_, c2)] = cT.v

                @block.vector
                def _(vector, dt_=dt_, c2=c2, psl=psl):
                    vector.wait_ge(semT, wo_mm[(dt_, c2)])
                    prev = (dt_ - 1, c2)
                    if prev in wo_st:
                        vector.wait_ge(semO, wo_st[prev])
                    vector.tensor_copy(swo[c2][:], ps_wo[:, psl])
                    vector.last.then_inc(semV, 1)
                    cV.inc()
                wo_ev[(dt_, c2)] = cV.v

                @block.sync
                def _(sync, dt_=dt_, c2=c2):
                    sync.wait_ge(semV, wo_ev[(dt_, c2)])
                    sync.dma_start(out=sT_dram[128 * dt_:128 * (dt_ + 1), 512 * c2:512 * (c2 + 1)],
                                   in_=swo[c2][:]).then_inc(semO, 16)
                    cO.inc(16)
                wo_st[(dt_, c2)] = cO.v

        barrier()
        wo.close()
        atn.close()
        att.close()
        maybe_stop(4)

        nc.leave_named_scope("P4_wout", _sid_P4_wout, False)
        _sid_P5_gateup = nc.enter_named_scope("P5_gateup", False)[0]
        gsc = contextlib.ExitStack()
        _stacks.append(gsc)
        g_store = gsc.enter_context(nc.sbuf_tensor("g_store", [128, 44 * SOWN], BF16))
        gu = contextlib.ExitStack()
        _stacks.append(gu)
        EG = gu.enter_context
        dec = [EG(nc.sbuf_tensor(f"dec{p}", [128, 16 * 512], BF16, side="right")) for p in range(2)]
        pkb = [EG(nc.sbuf_tensor(f"pkb{p}", [128, 4 * 512], U16, side="right")) for p in range(2)]
        scb = [EG(nc.sbuf_tensor(f"scb{p}", [128, 4 * 512], F16, side="right")) for p in range(2)]
        dqv = [EG(nc.sbuf_tensor(f"dqv{k}", [128, 2048], U16, side="right")) for k in range(4)]
        dqs = [EG(nc.sbuf_tensor(f"dqs{k}", [128, 2048], U16, side="right")) for k in range(4)]
        scq = EG(nc.sbuf_tensor("scq", [128, 2048], F16, side="right"))
        ps_gu = EG(nc.psum_tensor("ps_gu", [128, 1024], F32))

        gu_load, gu_dq, gu_mm, gu_ev = ({} for _ in range(4))

        for oc in range(22):
            pb = oc % 2
            is_gate_oc = 512 * (oc + 1) <= II

            @block.sync
            def _(sync, oc=oc, pb=pb):
                if oc >= 2:
                    sync.wait_ge(semV, gu_dq[oc - 2])
                sync.dma_start(out=_s3(pkb[pb][:], 4),
                               in_=_m3(pk16[:, 512 * oc:512 * (oc + 1)], 4)).then_inc(semL, 16)
                cL.inc(16)
                sync.dma_start(out=_s3(scb[pb][:], 4),
                               in_=_m3(screp[:, 512 * oc:512 * (oc + 1)], 4)).then_inc(semL, 16)
                cL.inc(16)
            gu_load[oc] = cL.v

            @block.vector
            def _(vector, oc=oc, pb=pb):
                vector.wait_ge(semL, gu_load[oc])
                if oc >= 2:
                    vector.wait_ge(semT, gu_mm[(oc - 2, 3, 1)])
                W = pkb[pb][:, 0:2048]
                vector.tensor_scalar(scq[:], scb[pb][:, 0:2048], 16384.0, None, ALU.mult)
                for kk in range(4):
                    sh = 9 - 4 * kk
                    if sh >= 0:
                        vector.tensor_scalar(dqv[kk][:], W, sh, 7 << 9,
                                             ALU.logical_shift_left, ALU.bitwise_and)
                    else:
                        vector.tensor_scalar(dqv[kk][:], W, -sh, 7 << 9,
                                             ALU.logical_shift_right, ALU.bitwise_and)
                for kk in range(4):
                    vector.tensor_scalar(dqs[kk][:], W, 4 * kk + 3, 15,
                                         ALU.logical_shift_right, ALU.logical_shift_left)
                for kk in range(4):
                    vector.tensor_tensor(dqv[kk][:], dqv[kk][:], dqs[kk][:], ALU.add)
                for kk in range(4):
                    vector.tensor_tensor(
                        dec[oc % 2][:, 2048 * kk: 2048 * (kk + 1)],
                        dqv[kk][:].bitcast(F16), scq[:], ALU.mult)
                vector.drain()
                vector.nop().then_inc(semV, 1)
                cV.inc()
            gu_dq[oc] = cV.v

            if is_gate_oc:
                for osub in range(4):
                    for c2 in range(2):
                        gu_ev[(oc, osub, c2)] = (semS, cS.v + osub * 2 + c2 + 1)

            for osub in range(4):
                o_glob = 512 * oc + 128 * osub
                for c2 in range(2):
                    par = c2
                    psl = slice(par * 512, (par + 1) * 512)

                    @block.tensor
                    def _(tensor, oc=oc, osub=osub, c2=c2, psl=psl):
                        tensor.wait_ge(semV, gu_dq[oc])
                        seqpos = osub * 2 + c2
                        if seqpos >= 2:
                            pk_ = (oc, (seqpos - 2) // 2, (seqpos - 2) % 2)
                        elif oc >= 1:
                            pk_ = (oc - 1, (seqpos + 6) // 2, (seqpos + 6) % 2)
                        else:
                            pk_ = None
                        if pk_ is not None:
                            sh, sv = gu_ev[pk_]
                            tensor.wait_ge(sh, sv)
                        for ht in range(16):
                            tensor.matmul(
                                ps_gu[:, psl],
                                dec[oc % 2][:, ht * 512 + 128 * osub: ht * 512 + 128 * osub + 128],
                                xnT[:, ht * SOWN + 512 * c2: ht * SOWN + 512 * c2 + 512],
                                start=(ht == 0), stop=(ht == 15))
                        tensor.last.then_inc(semT, 1)
                        cT.inc()
                    gu_mm[(oc, osub, c2)] = cT.v

                    if o_glob < II:
                        pass
                    else:
                        ut = (o_glob - II) // 128

                        @block.vector
                        def _(vector, oc=oc, osub=osub, c2=c2, psl=psl, ut=ut):
                            vector.wait_ge(semT, gu_mm[(oc, osub, c2)])
                            sl = slice(ut * SOWN + 512 * c2, ut * SOWN + 512 * c2 + 512)
                            vector.tensor_tensor(g_store[:, sl], g_store[:, sl],
                                                 ps_gu[:, psl], ALU.mult)
                            vector.last.then_inc(semV, 1)
                            cV.inc()
                        gu_ev[(oc, osub, c2)] = (semV, cV.v)

            if is_gate_oc:
                @block.scalar
                def _(scalar, oc=oc):
                    for osub in range(4):
                        for c2 in range(2):
                            gt = (512 * oc + 128 * osub) // 128
                            psl = slice(c2 * 512, (c2 + 1) * 512)
                            scalar.wait_ge(semT, gu_mm[(oc, osub, c2)])
                            aa = scalar.activation(
                                g_store[:, gt * SOWN + 512 * c2: gt * SOWN + 512 * c2 + 512],
                                ps_gu[:, psl], AF.Silu)
                            aa.then_inc(semS, 1)
                            cS.inc()
                            assert gu_ev[(oc, osub, c2)] == (semS, cS.v)

        barrier()
        if DEBUG_TAPS:
            @block.sync
            def _(sync):
                for gt in range(44):
                    sync.dma_start(out=h_dram[128 * gt:128 * (gt + 1), :],
                                   in_=g_store[:, gt * SOWN:(gt + 1) * SOWN]).then_inc(semO, 16)
                    cO.inc(16)
        gu.close()
        xn_stack.close()
        maybe_stop(5)

        nc.leave_named_scope("P5_gateup", _sid_P5_gateup, False)
        _sid_P7_down = nc.enter_named_scope("P7_down", False)[0]
        dn = contextlib.ExitStack()
        _stacks.append(dn)
        EN = dn.enter_context
        wdc = [EN(nc.sbuf_tensor(f"wdc{p}", [128, 44 * 128], BF16, side="right")) for p in range(2)]
        ysq = [EN(nc.sbuf_tensor(f"ysq{p}", [128, 512], BF16, side="right")) for p in range(2)]
        y_store = EN(nc.sbuf_tensor("y_store", [128, 16 * 512], F32, side="right"))
        srms2 = EN(nc.sbuf_tensor("srms2", [128, 512], F32, side="right"))
        rb2 = EN(nc.sbuf_tensor("rb2", [128, 512], F32, side="right"))
        sxt = [EN(nc.sbuf_tensor(f"sxt{p}", [128, 512], F32, side="right")) for p in range(4)]
        ssT = [EN(nc.sbuf_tensor(f"ssT{p}", [128, 512], F32, side="right")) for p in range(4)]
        comb = EN(nc.sbuf_tensor("comb", [128, 512], F32, side="right"))
        outb = [EN(nc.sbuf_tensor(f"outb{p}", [128, 512], F32, side="right")) for p in range(2)]
        ps_dn = EN(nc.psum_tensor("ps_dn", [128, 1024], F32))
        ps_ss2 = EN(nc.psum_tensor("ps_ss2", [128, 512], F32))

        dn_wload, dn_mm, dn_ysq, dn_ssmm, dn_ev, dn_ysq_s = ({} for _ in range(6))
        dn_xs_load, dn_comb, dn_ost = ({} for _ in range(3))

        for c2 in range(2):
            for ht in range(16):
                sq_ = c2 * 16 + ht
                wb = sq_ % 2
                pb = ht % 2
                psl = slice(pb * 512, (pb + 1) * 512)

                @block.gpsimd
                def _(gpsimd, c2=c2, ht=ht, wb=wb, sq_=sq_):
                    prevk = (sq_ - 2) if sq_ >= 2 else None
                    if prevk is not None:
                        gpsimd.wait_ge(semT, dn_mm[(prevk // 16, prevk % 16)])
                    gpsimd.dma_start(
                        out=_s3(wdc[wb][:], 44),
                        in_=_m3(wdn_dram[:, 128 * ht:128 * (ht + 1)], 44)).then_inc(semG, 16)
                    cG.inc(16)
                dn_wload[(c2, ht)] = cG.v

                @block.tensor
                def _(tensor, c2=c2, ht=ht, wb=wb, psl=psl):
                    tensor.wait_ge(semG, dn_wload[(c2, ht)])
                    prevk = (c2, ht - 2) if ht >= 2 else ((c2 - 1, 14 + ht) if c2 > 0 else None)
                    if prevk is not None and prevk in dn_ev:
                        tensor.wait_ge(semV, dn_ev[prevk])
                    for kt in range(44):
                        tensor.matmul(ps_dn[:, psl], wdc[wb][:, kt * 128:(kt + 1) * 128],
                                      g_store[:, kt * SOWN + 512 * c2: kt * SOWN + 512 * c2 + 512],
                                      start=(kt == 0), stop=(kt == 43))
                    tensor.last.then_inc(semT, 1)
                    cT.inc()
                dn_mm[(c2, ht)] = cT.v

                @block.scalar
                def _(scalar, c2=c2, ht=ht, psl=psl, pb=pb):
                    scalar.wait_ge(semT, dn_mm[(c2, ht)])
                    if ht > 1:
                        scalar.wait_ge(semT, dn_ssmm[(c2, ht - 2)])
                    scalar.activation(ysq[pb][:], ps_dn[:, psl], AF.Square)
                    scalar.last.then_inc(semS, 1)
                    cS.inc()
                dn_ysqs = cS.v

                @block.vector
                def _(vector, c2=c2, ht=ht, psl=psl):
                    vector.wait_ge(semT, dn_mm[(c2, ht)])
                    vector.tensor_copy(y_store[:, ht * 512:(ht + 1) * 512], ps_dn[:, psl])
                    vector.last.then_inc(semV, 1)
                    cV.inc()
                dn_ysq[(c2, ht)] = cV.v
                dn_ev[(c2, ht)] = dn_ysq[(c2, ht)]

                @block.tensor
                def _(tensor, c2=c2, ht=ht, dn_ysqs=dn_ysqs, pb=pb):
                    tensor.wait_ge(semS, dn_ysqs)
                    tensor.wait_ge(semV, dn_ysq[(c2, ht)])
                    tensor.matmul(ps_ss2[:, :], ones[:, :], ysq[pb][:],
                                  start=(ht == 0), stop=(ht == 15))
                    tensor.last.then_inc(semT, 1)
                    cT.inc()
                dn_ssmm[(c2, ht)] = cT.v

            @block.scalar
            def _(scalar, c2=c2):
                scalar.wait_ge(semT, dn_ssmm[(c2, 15)])
                scalar.activation(srms2[:], ps_ss2[:, :], AF.Sqrt, bias=c_feps, scale=1.0 / D)
                scalar.last.then_inc(semS, 1)
                cS.inc()
            rms_done = cS.v

            @block.vector
            def _(vector, c2=c2, rms_done=rms_done):
                vector.wait_ge(semS, rms_done)
                vector.reciprocal(rb2[:], srms2[:])
                vector.drain()
                for ht in range(16):
                    sl = slice(ht * 512, (ht + 1) * 512)
                    vector.tensor_tensor(y_store[:, sl], y_store[:, sl], rb2[:], ALU.mult)
                vector.last.then_inc(semV, 1)
                cV.inc()
            ynorm_done = cV.v

            @block.scalar
            def _(scalar, c2=c2, ynorm_done=ynorm_done):
                scalar.wait_ge(semV, ynorm_done)
                for ht in range(16):
                    sl = slice(ht * 512, (ht + 1) * 512)
                    scalar.activation(y_store[:, sl], y_store[:, sl], AF.Silu)
                scalar.last.then_inc(semS, 1)
                cS.inc()
            g_done = cS.v

            for ht in range(16):
                pb = ht % 2

                @block.sync
                def _(sync, c2=c2, ht=ht):
                    prevk = (c2, ht - 4) if ht >= 4 else ((c2 - 1, 12 + ht) if c2 > 0 else None)
                    if prevk is not None and prevk in dn_comb:
                        sync.wait_ge(semV, dn_comb[prevk])
                    sync.dma_start(out=sxt[ht % 4][:],
                                   in_=xT[128 * ht:128 * (ht + 1),
                                          HALO + 512 * c2: HALO + 512 * c2 + 512]
                                   ).then_inc(semL, 16)
                    cL.inc(16)
                    sync.dma_start(out=ssT[ht % 4][:],
                                   in_=sT_dram[128 * ht:128 * (ht + 1), 512 * c2:512 * (c2 + 1)]
                                   ).then_inc(semL, 16)
                    cL.inc(16)
                dn_xs_load[(c2, ht)] = cL.v

                @block.vector
                def _(vector, c2=c2, ht=ht, pb=pb, g_done=g_done):
                    vector.wait_ge(semS, g_done)
                    vector.wait_ge(semL, dn_xs_load[(c2, ht)])
                    prevk = (c2, ht - 2) if ht >= 2 else ((c2 - 1, 14 + ht) if c2 > 0 else None)
                    if prevk is not None and prevk in dn_ost:
                        vector.wait_ge(semO, dn_ost[prevk])
                    sl = slice(ht * 512, (ht + 1) * 512)
                    vector.tensor_tensor(comb[:], ssT[ht % 4][:], y_store[:, sl], ALU.mult)
                    vector.drain()
                    vector.tensor_tensor(outb[pb][:], sxt[ht % 4][:], comb[:], ALU.add)
                    vector.last.then_inc(semV, 1)
                    cV.inc()
                dn_comb[(c2, ht)] = cV.v

                @block.sync
                def _(sync, c2=c2, ht=ht, pb=pb):
                    sync.wait_ge(semV, dn_comb[(c2, ht)])
                    sync.dma_start(out=o_out[128 * ht:128 * (ht + 1), 512 * c2:512 * (c2 + 1)],
                                   in_=outb[pb][:]).then_inc(semO, 16)
                    cO.inc(16)
                dn_ost[(c2, ht)] = cO.v

        @block.sync
        def _(sync):
            sync.wait_ge(semO, cO.v)

        nc.leave_named_scope("P7_down", _sid_P7_down, False)
        dn.close()
        gsc.close()

    return nc


def kernel(L0_attn_x, L0_attn_w_qkv, L0_attn_w_out, L0_attn_q_norm_w, L0_attn_k_norm_w,
           L0_ffn_gate_up_packed, L0_ffn_gate_up_scales, L0_ffn_down_packed,
           L0_ffn_down_scales):
    SD, fD = _perm_d()
    SI, fI = _perm_i()

    x = np.asarray(L0_attn_x, dtype=np.float32)
    wqkv = np.asarray(L0_attn_w_qkv, dtype=np.float32)
    wout = np.asarray(L0_attn_w_out, dtype=np.float32)
    qnw = np.asarray(L0_attn_q_norm_w, dtype=np.float32)
    knw = np.asarray(L0_attn_k_norm_w, dtype=np.float32)
    gup = np.asarray(L0_ffn_gate_up_packed)
    gus = np.asarray(L0_ffn_gate_up_scales)
    dnp_ = np.asarray(L0_ffn_down_packed)
    dns = np.asarray(L0_ffn_down_scales)

    wqkvT_st = np.ascontiguousarray(wqkv.T[SD, :])
    woutT_st = np.ascontiguousarray(wout.T[:, SD])
    wqpp_st = np.ascontiguousarray(np.tile(qnw, 2)[:, None])
    wkpp_st = np.ascontiguousarray(np.tile(knw, 2)[:, None])

    cols = np.concatenate([SI, II + SI])
    pk16_st = np.ascontiguousarray(gup.view('<u2').reshape(2 * II, 512)[cols, :].T)
    screp_st = np.ascontiguousarray(np.repeat(gus.astype(np.float16), 32, axis=0)[:, cols])
    dpk16_st = np.ascontiguousarray(dnp_.view('<u2').reshape(D, 1408)[SD, :].T)
    dscrep_st = np.ascontiguousarray(np.repeat(dns.astype(np.float16), 32, axis=0)[:, SD])

    eye_st = np.eye(128, dtype=np.float32).astype(bfdt)
    bd_st = np.zeros((128, 128), np.float32)
    bd_st[:64, :64] = 1.0
    bd_st[64:, 64:] = 1.0
    bd_st = bd_st.astype(bfdt)
    ones_st = np.ones((128, 128), np.float32).astype(bfdt)
    consts_st = np.zeros((128, 8), np.float32)
    consts_st[:, 0] = 0.5
    consts_st[:, 1] = 64.0 * 1e-6
    consts_st[:, 2] = 1e-6
    consts_st[:, 3] = 1e-5
    consts_st[:, 4] = 1e-6

    i_ = np.arange(128)[:, None]
    jj = np.arange(640)[None, :]
    win = (jj > i_) & (jj <= i_ + 512)
    mask_by_half = {}
    for half in (0, 1):
        m = np.zeros((8, 128, 640), np.float32)
        for t in range(8):
            ok = win.copy()
            if half == 0:
                ok = ok & ((128 * t + jj) >= 512)
            m[t] = np.where(ok, 0.0, NEG)
        mask_by_half[half] = np.ascontiguousarray(
            m.transpose(1, 0, 2).reshape(128, 8 * 640)).astype(bfdt)

    invf = 1.0 / (10000.0 ** (np.arange(0, HD, 2, dtype=np.float64) / HD))
    rope_by_half = {}
    for half in (0, 1):
        q0 = SOWN * half
        pos = np.maximum(q0 - HALO + np.arange(SK, dtype=np.float64), 0.0)
        ang = pos[None, :] * invf[:, None]
        ang2 = np.concatenate([ang, ang], axis=0)
        cos = np.concatenate([np.cos(ang2)] * 2, axis=0).astype(np.float32)
        sin = np.concatenate([np.sin(ang2)] * 2, axis=0).astype(np.float32)
        rope_by_half[half] = (np.ascontiguousarray(cos), np.ascontiguousarray(sin))

    in_maps = []
    for c in range(8):
        b, half = c // 2, c % 2
        q0 = SOWN * half
        xb = x[b].T[SD, :]
        xs = np.zeros((D, SK), np.float32)
        lo = q0 - HALO
        src_lo = max(lo, 0)
        xs[:, src_lo - lo:] = xb[:, src_lo:q0 + SOWN]
        cos_st, sin_st = rope_by_half[half]
        in_maps.append(dict(
            xT=np.ascontiguousarray(xs), wqkvT=wqkvT_st, woutT=woutT_st,
            wqpp=wqpp_st, wkpp=wkpp_st, pk16=pk16_st, screp=screp_st,
            dpk16=dpk16_st, dscrep=dscrep_st, masks_in=mask_by_half[half],
            cos_in=cos_st, sin_in=sin_st, eye_in=eye_st, bd_in=bd_st,
            ones_in=ones_st, consts_in=consts_st,
        ))

    nc = build_nc()
    res = run_bass_kernel_spmd(nc, in_maps, core_ids=list(range(8)))

    out = np.empty((4, 2048, D), np.float32)
    for c in range(8):
        b, half = c // 2, c % 2
        o = res.results[c]["o_out"]
        out[b, SOWN * half:SOWN * (half + 1), :] = o[fD, :].T
    if DEBUG_TAPS:
        kernel.last_results = res
    return out


# version 31
# speedup vs baseline: 1.2134x; 1.0567x over previous
import contextlib
import numpy as np
import ml_dtypes
import concourse.bass as bass
import concourse.mybir as mybir
from concourse.bass_utils import run_bass_kernel_spmd

F32 = mybir.dt.float32
BF16 = mybir.dt.bfloat16
F16 = mybir.dt.float16
U16 = mybir.dt.uint16
AF = mybir.ActivationFunctionType
ALU = mybir.AluOpType
bfdt = ml_dtypes.bfloat16

D = 2048
NQH, NKV, HD = 32, 8, 64
SK = 1536
SOWN = 1024
HALO = 512
II = 5632
NEG = -1e30

DEBUG_TAPS = False
import os
PHASES = int(os.environ.get("KPH", "9"))


class _StopEmit(Exception):
    pass


_SKIP_TRACK = {"wait_ge", "nop", "drain", "maybe_wait_ge", "sem_inc", "wait_op"}


class _T:

    def __init__(self, eng):
        self._e = eng
        self.last = None

    def __getattr__(self, k):
        f = getattr(self._e, k)
        if callable(f) and not k.startswith("_"):
            def g(*a, _f=f, _k=k, **kw):
                r = _f(*a, **kw)
                if _k not in _SKIP_TRACK and r is not None:
                    self.last = r
                return r
            return g
        return f


def _wrap_block(block):
    class W:
        def __getattr__(self, name):
            dec = getattr(block, name)

            def mydec(fn):
                def inner(eng):
                    return fn(_T(eng))
                inner.__name__ = getattr(fn, "__name__", "_")
                return dec(inner)
            return mydec
    return W()


def _perm_d():
    h = np.arange(D)
    f = (h % 4) * 512 + (h // 8) * 2 + ((h // 4) % 2)
    s = np.empty(D, np.int64)
    s[f] = h
    return s, f


def _perm_i():
    i = np.arange(II)
    f = (i % 4) * 1408 + (i // 8) * 2 + ((i // 4) % 2)
    s = np.empty(II, np.int64)
    s[f] = i
    return s, f


class Ctr:
    def __init__(self):
        self.v = 0

    def inc(self, n=1):
        self.v += n
        return self.v


def _m3(ap, a):
    return ap.rearrange("(a p) c -> p a c", p=128)


def _s3(ap, a):
    return ap.rearrange("p (a c) -> p a c", a=a)


def build_nc():
    try:
        return _build_nc_inner()
    except _StopEmit as e:
        return e.nc


def _build_nc_inner():
    nc = bass.Bass("TRN2", target_bir_lowering=False, debug=False)

    xT = nc.dram_tensor("xT", [D, SK], F32, kind="ExternalInput").ap()
    wqkvT = nc.dram_tensor("wqkvT", [D, 3072], F32, kind="ExternalInput").ap()
    woutT = nc.dram_tensor("woutT", [D, D], F32, kind="ExternalInput").ap()
    wqpp = nc.dram_tensor("wqpp", [128, 1], F32, kind="ExternalInput").ap()
    wkpp = nc.dram_tensor("wkpp", [128, 1], F32, kind="ExternalInput").ap()
    pk16 = nc.dram_tensor("pk16", [512, 11264], U16, kind="ExternalInput").ap()
    screp = nc.dram_tensor("screp", [512, 11264], F16, kind="ExternalInput").ap()
    dpk16 = nc.dram_tensor("dpk16", [1408, D], U16, kind="ExternalInput").ap()
    dscrep = nc.dram_tensor("dscrep", [1408, D], F16, kind="ExternalInput").ap()
    masks_in = nc.dram_tensor("masks_in", [128, 8 * 640], BF16, kind="ExternalInput").ap()
    cos_in = nc.dram_tensor("cos_in", [128, SK], BF16, kind="ExternalInput").ap()
    sin_in = nc.dram_tensor("sin_in", [128, SK], BF16, kind="ExternalInput").ap()
    eye_in = nc.dram_tensor("eye_in", [128, 128], BF16, kind="ExternalInput").ap()
    bd_in = nc.dram_tensor("bd_in", [128, 128], BF16, kind="ExternalInput").ap()
    ones_in = nc.dram_tensor("ones_in", [128, 128], BF16, kind="ExternalInput").ap()
    consts_in = nc.dram_tensor("consts_in", [128, 8], F32, kind="ExternalInput").ap()

    o_out = nc.dram_tensor("o_out", [D, SOWN], F32, kind="ExternalOutput").ap()
    sk_ = "ExternalOutput" if DEBUG_TAPS else "Internal"
    sT_dram = nc.dram_tensor("sT_dram", [D, SOWN], F32, kind=sk_).ap()
    wdn_dram = nc.dram_tensor("wdn_dram", [II, D], BF16, kind=sk_).ap()
    if DEBUG_TAPS:
        xn_dram = nc.dram_tensor("xn_dram", [D, SK], BF16, kind="ExternalOutput").ap()
        at_dram = nc.dram_tensor("at_dram", [D, SOWN], BF16, kind="ExternalOutput").ap()
        h_dram = nc.dram_tensor("h_dram", [II, SOWN], BF16, kind="ExternalOutput").ap()

    es = contextlib.ExitStack()
    _stacks = []
    with es:
        E = es.enter_context

        semL = E(nc.semaphore("semL"))
        semG = E(nc.semaphore("semG"))
        semV = E(nc.semaphore("semV"))
        semS = E(nc.semaphore("semS"))
        semT = E(nc.semaphore("semT"))
        semB = E(nc.semaphore("semB"))
        semO = E(nc.semaphore("semO"))
        cL, cG, cV, cS, cT, cB, cO = (Ctr() for _ in range(7))

        block = _wrap_block(E(nc.Block()))

        def barrier():
            tgt = cB.inc(5)
            for name in ("sync", "gpsimd", "vector", "scalar", "tensor"):
                @getattr(block, name)
                def _(eng, name=name):
                    if name != "sync":
                        eng.drain()
                    eng.nop().then_inc(semB, 1)
                    eng.wait_ge(semB, tgt)


        def maybe_stop(n):
            if PHASES <= n:
                @block.sync
                def _(sync):
                    sync.wait_ge(semO, cO.v)
                for s in reversed(_stacks):
                    try:
                        s.close()
                    except Exception:
                        pass
                err = _StopEmit()
                err.nc = nc
                raise err

        consts = E(nc.sbuf_tensor("consts", [128, 8], F32))
        eye = E(nc.sbuf_tensor("eye", [128, 128], BF16))
        bd64 = E(nc.sbuf_tensor("bd64", [128, 128], BF16))
        ones = E(nc.sbuf_tensor("ones", [128, 128], BF16))
        swq = E(nc.sbuf_tensor("swq", [128, 1], F32))
        swk = E(nc.sbuf_tensor("swk", [128, 1], F32))

        xn_stack = contextlib.ExitStack()
        _stacks.append(xn_stack)
        xnT = xn_stack.enter_context(nc.sbuf_tensor("xnT", [128, 16 * SOWN], BF16, side="right"))

        atn = contextlib.ExitStack()
        _stacks.append(atn)

        att = contextlib.ExitStack()
        _stacks.append(att)
        EA = att.enter_context
        masks = EA(nc.sbuf_tensor("masks", [128, 8 * 640], BF16))
        qhat = EA(nc.sbuf_tensor("qhat", [128, 16 * SOWN], BF16))
        khat = EA(nc.sbuf_tensor("khat", [128, 8 * SK], BF16))
        vst = EA(nc.sbuf_tensor("vst", [128, 12 * 512], BF16))
        rope_sc = contextlib.ExitStack()
        _stacks.append(rope_sc)
        ER = rope_sc.enter_context
        cosT = ER(nc.sbuf_tensor("cosT", [128, SK], BF16))
        sinT = ER(nc.sbuf_tensor("sinT", [128, SK], BF16))
        xnH = ER(nc.sbuf_tensor("xnH", [128, 16 * HALO], BF16))

        @block.sync
        def _(sync):
            for dst, src in ((consts[:], consts_in), (eye[:], eye_in),
                             (bd64[:], bd_in), (ones[:], ones_in),
                             (swq[:], wqpp), (swk[:], wkpp),
                             (masks[:], masks_in), (cosT[:], cos_in),
                             (sinT[:], sin_in)):
                sync.dma_start(out=dst, in_=src).then_inc(semL, 16)
                cL.inc(16)

        if os.environ.get("KBAR"):
            barrier()
        maybe_stop(0)

        c_half = consts[:, 0:1]
        c_qeps = consts[:, 1:2]
        c_keps = consts[:, 2:3]
        c_lneps = consts[:, 3:4]
        c_feps = consts[:, 4:5]

        _sid_P1_ln = nc.enter_named_scope("P1_ln", False)[0]
        ln = contextlib.ExitStack()
        _stacks.append(ln)
        EL = ln.enter_context
        xin = [EL(nc.sbuf_tensor(f"xin{p}", [128, 16 * 256], F32, side="right")) for p in range(2)]
        xbb_ = EL(nc.sbuf_tensor("xbb", [128, 16 * 256], BF16, side="right"))
        xsq_ = EL(nc.sbuf_tensor("xsq", [128, 16 * 256], BF16, side="right"))
        xbb = [xbb_, xbb_]
        xsq = [xsq_, xsq_]
        xm2 = [EL(nc.sbuf_tensor(f"xm2{q}", [128, 256], F32, side="right")) for q in range(4)]
        mtile = EL(nc.sbuf_tensor("mtile", [128, 256], F32, side="right"))
        m2tile = EL(nc.sbuf_tensor("m2tile", [128, 256], F32, side="right"))
        vtile = EL(nc.sbuf_tensor("vtile", [128, 256], F32, side="right"))
        stile = EL(nc.sbuf_tensor("stile", [128, 256], F32, side="right"))
        rtile = EL(nc.sbuf_tensor("rtile", [128, 256], F32, side="right"))
        ps_mb = EL(nc.psum_tensor("ps_mb", [128, 1024], F32))
        ps_sq = EL(nc.psum_tensor("ps_sq", [128, 1024], F32))

        ln_load_done, ln_ev_done, ln_sq_done, ln_mm_done = [], [], [], []
        ln_var_done, ln_std_done = [], []

        KLNC = int(os.environ.get("KLNC", "6"))
        KLNS = int(os.environ.get("KLNS", "9"))
        for c in range(KLNC):
            p = c % 2

            @block.sync
            def _(sync, c=c, p=p):
                if c >= 2:
                    sync.wait_ge(semV, ln_ev_done[c - 2])
                sync.dma_start(out=_s3(xin[p][:], 16),
                               in_=_m3(xT[:, 256 * c:256 * (c + 1)], 16)).then_inc(semL, 16)
                cL.inc(16)
            ln_load_done.append(cL.v)

            @block.vector
            def _(vector, c=c, p=p):
                if KLNS < 2:
                    return
                vector.wait_ge(semL, ln_load_done[c])
                if c >= 1:
                    vector.wait_ge(semT, ln_mm_done[c - 1])
                for rt in range(16):
                    sl = slice(rt * 256, (rt + 1) * 256)
                    vector.tensor_copy(xbb[p][:, sl], xin[p][:, sl])
                for rt in range(16):
                    sl = slice(rt * 256, (rt + 1) * 256)
                    vector.tensor_tensor(xsq[p][:, sl], xbb[p][:, sl], xbb[p][:, sl], ALU.mult)
                vector.last.then_inc(semV, 1)
                cV.inc()
            ln_sq_done.append(cV.v)

            @block.tensor
            def _(tensor, c=c, p=p):
                if KLNS < 3:
                    return
                tensor.wait_ge(semV, ln_sq_done[c])
                if c >= 2:
                    tensor.wait_ge(semV, ln_var_done[c - 2])
                psl = slice(p * 512, p * 512 + 256)
                for rt in range(16):
                    sl = slice(rt * 256, (rt + 1) * 256)
                    tensor.matmul(ps_mb[:, psl], ones[:, :], xbb[p][:, sl],
                                  start=(rt == 0), stop=(rt == 15))
                for rt in range(16):
                    sl = slice(rt * 256, (rt + 1) * 256)
                    tensor.matmul(ps_sq[:, psl], ones[:, :], xsq[p][:, sl],
                                  start=(rt == 0), stop=(rt == 15))
                tensor.last.then_inc(semT, 1)
                cT.inc()
            ln_mm_done.append(cT.v)

            @block.vector
            def _(vector, c=c, p=p):
                if KLNS < 4:
                    return
                vector.wait_ge(semT, ln_mm_done[c])
                psl = slice(p * 512, p * 512 + 256)
                vector.tensor_scalar(mtile[:], ps_mb[:, psl], 1.0 / D, None, ALU.mult)
                vector.drain()
                vector.tensor_tensor(m2tile[:], mtile[:], mtile[:], ALU.mult)
                vector.drain()
                vector.scalar_tensor_tensor(vtile[:], ps_sq[:, psl], 1.0 / D, m2tile[:],
                                            ALU.mult, ALU.subtract)
                vector.last.then_inc(semV, 1)
                cV.inc()
            ln_var_done.append(cV.v)

            @block.scalar
            def _(scalar, c=c):
                if KLNS < 5:
                    return
                scalar.wait_ge(semV, ln_var_done[c])
                scalar.activation(stile[:], vtile[:], AF.Sqrt, bias=c_lneps, scale=1.0)
                scalar.last.then_inc(semS, 1)
                cS.inc()
            ln_std_done.append(cS.v)

            @block.vector
            def _(vector, c=c, p=p):
                if KLNS < 6:
                    return
                vector.wait_ge(semS, ln_std_done[c])
                vector.reciprocal(rtile[:], stile[:])
                vector.drain()
                def _dst(rt):
                    if c < 2:
                        return xnH[:, rt * HALO + 256 * c: rt * HALO + 256 * (c + 1)]
                    j0 = 256 * c - 512
                    return xnT[:, rt * SOWN + j0: rt * SOWN + j0 + 256]
                for rt0 in range(0, 16, 4):
                    for q in range(4):
                        sl = slice((rt0 + q) * 256, (rt0 + q + 1) * 256)
                        vector.tensor_tensor(xm2[q][:], xin[p][:, sl], mtile[:], ALU.subtract)
                    for q in range(4):
                        vector.tensor_tensor(_dst(rt0 + q), xm2[q][:], rtile[:], ALU.mult)
                vector.last.then_inc(semV, 1)
                cV.inc()
            ln_ev_done.append(cV.v)

        barrier()
        if DEBUG_TAPS:
            @block.sync
            def _(sync):
                for rt in range(16):
                    sync.dma_start(out=xn_dram[128 * rt:128 * (rt + 1), 0:HALO],
                                   in_=xnH[:, rt * HALO:(rt + 1) * HALO]).then_inc(semO, 16)
                    cO.inc(16)
                    sync.dma_start(out=xn_dram[128 * rt:128 * (rt + 1), HALO:SK],
                                   in_=xnT[:, rt * SOWN:(rt + 1) * SOWN]).then_inc(semO, 16)
                    cO.inc(16)
        ln.close()
        maybe_stop(1)


        nc.leave_named_scope("P1_ln", _sid_P1_ln, False)
        _sid_P2_qkv = nc.enter_named_scope("P2_qkv", False)[0]
        qk = contextlib.ExitStack()
        _stacks.append(qk)
        EQ = qk.enter_context
        wcol = [EQ(nc.sbuf_tensor(f"wcol{p}", [128, 16 * 128], BF16, side="right")) for p in range(2)]
        sqscr2 = [EQ(nc.sbuf_tensor(f"sqscr{p}", [128, 512], BF16, side="right")) for p in range(3)]
        sqs2 = [EQ(nc.sbuf_tensor(f"sqs{p}", [128, 512], F32, side="right")) for p in range(3)]
        rqs = EQ(nc.sbuf_tensor("rqs", [128, 512], F32, side="right"))
        qns = EQ(nc.sbuf_tensor("qns", [128, 512], F32, side="right"))
        rots = EQ(nc.sbuf_tensor("rots", [128, 512], F32, side="right"))
        t1s = EQ(nc.sbuf_tensor("t1s", [128, 512], F32, side="right"))
        t2s = EQ(nc.sbuf_tensor("t2s", [128, 512], F32, side="right"))
        vtmps = EQ(nc.sbuf_tensor("vtmps", [128, 512], BF16, side="right"))
        ps_qkv = EQ(nc.psum_tensor("ps_qkv", [128, 2048], F32))
        ps_ss = EQ(nc.psum_tensor("ps_ss", [128, 1536], F32))
        ps_vt = EQ(nc.psum_tensor("ps_vt", [128, 1024], BF16))

        dd = contextlib.ExitStack()
        _stacks.append(dd)
        ED = dd.enter_context
        HC = 1024
        dpkb = [ED(nc.sbuf_tensor(f"dpkb{p}", [128, HC], U16, side="right")) for p in range(2)]
        dscb = [ED(nc.sbuf_tensor(f"dscb{p}", [128, HC], F16, side="right")) for p in range(2)]
        ddv = [ED(nc.sbuf_tensor(f"ddv{k}", [128, HC], U16, side="right")) for k in range(4)]
        dds = [ED(nc.sbuf_tensor(f"dds{k}", [128, HC], U16, side="right")) for k in range(4)]
        dscq = ED(nc.sbuf_tensor("dscq", [128, HC], F16, side="right"))
        ddec = [[ED(nc.sbuf_tensor(f"ddec{p}_{k}", [128, HC], BF16, side="right")) for k in range(4)]
                for p in range(2)]
        dd_load, dd_dec, dd_st = ({} for _ in range(3))

        def emit_dd_load(hc):
            rt, hf = hc // 2, hc % 2
            pb = hc % 2

            @block.sync
            def _(sync, rt=rt, hf=hf, pb=pb, hc=hc):
                if hc >= 2:
                    sync.wait_ge(semV, dd_dec[hc - 2])
                sync.dma_start(out=dpkb[pb][:],
                               in_=dpk16[128 * rt:128 * (rt + 1), HC * hf:HC * (hf + 1)]
                               ).then_inc(semL, 16)
                cL.inc(16)
                sync.dma_start(out=dscb[pb][:],
                               in_=dscrep[128 * rt:128 * (rt + 1), HC * hf:HC * (hf + 1)]
                               ).then_inc(semL, 16)
                cL.inc(16)
            dd_load[hc] = cL.v

        def emit_dd_decode(hc):
            rt, hf = hc // 2, hc % 2
            pb = hc % 2

            @block.vector
            def _(vector, rt=rt, hf=hf, pb=pb, hc=hc):
                vector.wait_ge(semL, dd_load[hc])
                if hc >= 2:
                    vector.wait_ge(semO, dd_st[hc - 2])
                W = dpkb[pb][:]
                vector.tensor_scalar(dscq[:], dscb[pb][:], 16384.0, None, ALU.mult)
                for kk in range(4):
                    sh = 9 - 4 * kk
                    if sh >= 0:
                        vector.tensor_scalar(ddv[kk][:], W, sh, 7 << 9,
                                             ALU.logical_shift_left, ALU.bitwise_and)
                    else:
                        vector.tensor_scalar(ddv[kk][:], W, -sh, 7 << 9,
                                             ALU.logical_shift_right, ALU.bitwise_and)
                for kk in range(4):
                    vector.tensor_scalar(dds[kk][:], W, 4 * kk + 3, 15,
                                         ALU.logical_shift_right, ALU.logical_shift_left)
                for kk in range(4):
                    vector.tensor_tensor(ddv[kk][:], ddv[kk][:], dds[kk][:], ALU.add)
                for kk in range(4):
                    vector.tensor_tensor(ddec[pb][kk][:], ddv[kk][:].bitcast(F16),
                                         dscq[:], ALU.mult)
                vector.last.then_inc(semV, 1)
                cV.inc()
            dd_dec[hc] = cV.v

            @block.sync
            def _(sync, rt=rt, hf=hf, pb=pb, hc=hc):
                sync.wait_ge(semV, dd_dec[hc])
                for kk in range(4):
                    it = 11 * kk + rt
                    sync.dma_start(out=wdn_dram[128 * it:128 * (it + 1), HC * hf:HC * (hf + 1)],
                                   in_=ddec[pb][kk][:]).then_inc(semO, 16)
                    cO.inc(16)
            dd_st[hc] = cO.v

        def rhs_xn(kt, ch):
            if ch == 0:
                return xnH[:, kt * HALO: kt * HALO + 512]
            j0 = 512 * (ch - 1)
            return xnT[:, kt * SOWN + j0: kt * SOWN + j0 + 512]

        qk_wload = {}
        qk_mm = {}
        qk_ev = {}
        qk_ssmm = {}
        qk_sqrt = {}
        qk_vtp = {}
        qk_vtmp = {}

        seq2 = []
        for ot in range(24):
            for ch in ([0, 1, 2] if ot >= 16 else [1, 2]):
                seq2.append((ot, ch))

        prev_on_parity = {}
        prev_ev = None
        prev_v = None
        qk_pos = [p for p in seq2 if p[0] < 20]
        NQK = len(qk_pos)
        sc_order = [("sq", 0)]
        for k in range(1, NQK):
            sc_order += [("sq", k), ("sqrt", k - 1)]
        sc_order.append(("sqrt", NQK - 1))
        cS_qk0 = cS.v
        qk_sq_s = {}
        for idx, (kind, k) in enumerate(sc_order):
            if kind == "sq":
                qk_sq_s[qk_pos[k]] = cS_qk0 + idx + 1
            else:
                qk_sqrt[qk_pos[k]] = cS_qk0 + idx + 1
        for ot in range(24):
            if ot >= 2:
                emit_dd_decode(ot - 2)
            if ot < 22:
                emit_dd_load(ot)

            @block.gpsimd
            def _(gpsimd, ot=ot):
                if ot >= 2:
                    gpsimd.wait_ge(semT, qk_mm[(ot - 2, 2)])
                gpsimd.dma_start(
                    out=_s3(wcol[ot % 2][:], 16),
                    in_=_m3(wqkvT[:, 128 * ot:128 * (ot + 1)], 16)).then_inc(semG, 16)
                cG.inc(16)
            qk_wload[ot] = cG.v

            for ch in ([0, 1, 2] if ot >= 16 else [1, 2]):
                pos = seq2.index((ot, ch))
                par = pos % 4
                psl = slice(par * 512, (par + 1) * 512)

                @block.tensor
                def _(tensor, ot=ot, ch=ch, par=par, psl=psl):
                    tensor.wait_ge(semG, qk_wload[ot])
                    if par in prev_on_parity:
                        tensor.wait_ge(semV, qk_ev[prev_on_parity[par]])
                    for kt in range(16):
                        tensor.matmul(ps_qkv[:, psl], wcol[ot % 2][:, kt * 128:(kt + 1) * 128],
                                      rhs_xn(kt, ch), start=(kt == 0), stop=(kt == 15))
                    tensor.last.then_inc(semT, 1)
                    cT.inc()
                qk_mm[(ot, ch)] = cT.v
                prev_on_parity[par] = (ot, ch)

                if ot < 20:
                    is_q = ot < 16
                    kq = qk_pos.index((ot, ch))
                    sssl = slice((kq % 3) * 512, (kq % 3) * 512 + 512)

                    @block.tensor
                    def _(tensor, ot=ot, ch=ch, kq=kq, sssl=sssl):
                        tensor.wait_ge(semS, qk_sq_s[(ot, ch)])
                        if kq >= 3:
                            tensor.wait_ge(semS, qk_sqrt[qk_pos[kq - 3]])
                        tensor.matmul(ps_ss[:, sssl], bd64[:, :], sqscr2[kq % 3][:],
                                      start=True, stop=True)
                        tensor.last.then_inc(semT, 1)
                        cT.inc()
                    qk_ssmm[(ot, ch)] = cT.v

                    @block.vector
                    def _(vector, ot=ot, ch=ch, psl=psl, is_q=is_q, kq=kq):
                        vector.wait_ge(semS, qk_sqrt[(ot, ch)])
                        vector.reciprocal(rqs[:], sqs2[kq % 3][:])
                        vector.drain()
                        vector.scalar_tensor_tensor(qns[:], ps_qkv[:, psl],
                                                    (swq[:, 0:1] if is_q else swk[:, 0:1]),
                                                    rqs[:], ALU.mult, ALU.mult)
                        vector.drain()
                        for hb in range(2):
                            b0 = hb * 64
                            vector.tensor_scalar(rots[b0:b0 + 32, :], qns[b0 + 32:b0 + 64, :],
                                                 -1.0, None, ALU.mult)
                            vector.tensor_copy(rots[b0 + 32:b0 + 64, :], qns[b0:b0 + 32, :])
                        j0 = 512 * ch
                        cs = cosT[:, j0:j0 + 512]
                        sn = sinT[:, j0:j0 + 512]
                        vector.tensor_tensor(t1s[:], qns[:], cs, ALU.mult)
                        vector.tensor_tensor(t2s[:], rots[:], sn, ALU.mult)
                        vector.drain()
                        if is_q:
                            jq = 512 * (ch - 1)
                            dst = qhat[:, ot * SOWN + jq: ot * SOWN + jq + 512]
                            vector.tensor_tensor(dst, t1s[:], t2s[:], ALU.add)
                        else:
                            gA = 2 * (ot - 16)
                            jk = 512 * ch
                            for (gg, pr) in ((gA, slice(0, 64)), (gA + 1, slice(64, 128))):
                                for half in (slice(0, 64), slice(64, 128)):
                                    vector.tensor_tensor(
                                        khat[half, gg * SK + jk: gg * SK + jk + 512],
                                        t1s[pr, :], t2s[pr, :], ALU.add)
                        vector.last.then_inc(semV, 1)
                        cV.inc()
                    qk_ev[(ot, ch)] = cV.v
                    prev_ev = cV.v
                else:
                    @block.vector
                    def _(vector, ot=ot, ch=ch, psl=psl, prev_v=prev_v):
                        vector.wait_ge(semT, qk_mm[(ot, ch)])
                        if prev_v is not None:
                            vector.wait_ge(semT, qk_vtp[prev_v])
                        vector.tensor_copy(vtmps[:], ps_qkv[:, psl])
                        vector.last.then_inc(semV, 1)
                        cV.inc()
                    qk_vtmp[(ot, ch)] = cV.v

                    @block.tensor
                    def _(tensor, ot=ot, ch=ch):
                        tensor.wait_ge(semV, qk_vtmp[(ot, ch)])
                        for u in range(4):
                            tensor.transpose(ps_vt[:, u * 128:(u + 1) * 128],
                                             vtmps[:, u * 128:(u + 1) * 128], eye[:, :])
                        tensor.last.then_inc(semT, 1)
                        cT.inc()
                    qk_vtp[(ot, ch)] = cT.v
                    prev_v = (ot, ch)

                    @block.vector
                    def _(vector, ot=ot, ch=ch):
                        vector.wait_ge(semT, qk_vtp[(ot, ch)])
                        for u in range(4):
                            ktile = ch * 4 + u
                            dst = vst[:, ktile * 512 + 128 * (ot - 20):
                                      ktile * 512 + 128 * (ot - 20) + 128]
                            vector.tensor_copy(dst, ps_vt[:, u * 128:(u + 1) * 128])
                        vector.last.then_inc(semV, 1)
                        cV.inc()
                    qk_ev[(ot, ch)] = cV.v

        @block.scalar
        def _(scalar):
            for kind, k in sc_order:
                ot, ch = qk_pos[k]
                is_q = ot < 16
                par = seq2.index((ot, ch)) % 4
                psl = slice(par * 512, (par + 1) * 512)
                sssl = slice((k % 3) * 512, (k % 3) * 512 + 512)
                if kind == "sq":
                    scalar.wait_ge(semT, qk_mm[(ot, ch)])
                    aa = scalar.activation(sqscr2[k % 3][:], ps_qkv[:, psl], AF.Square)
                else:
                    scalar.wait_ge(semT, qk_ssmm[(ot, ch)])
                    if k >= 3:
                        scalar.wait_ge(semV, qk_ev[qk_pos[k - 3]])
                    aa = scalar.activation(sqs2[k % 3][:], ps_ss[:, sssl], AF.Sqrt,
                                           bias=(c_qeps if is_q else c_keps),
                                           scale=(1.0 if is_q else 1.0 / 64))
                aa.then_inc(semS, 1)
                cS.inc()
        assert cS.v == cS_qk0 + 2 * NQK

        emit_dd_decode(20)
        emit_dd_decode(21)
        barrier()
        dd.close()
        qk.close()
        rope_sc.close()
        attnT = atn.enter_context(nc.sbuf_tensor("attnT", [128, 16 * SOWN], BF16))
        maybe_stop(2)

        nc.leave_named_scope("P2_qkv", _sid_P2_qkv, False)
        _sid_P3_attn = nc.enter_named_scope("P3_attn", False)[0]
        at = contextlib.ExitStack()
        _stacks.append(at)
        ET = at.enter_context
        ptil = [[ET(nc.sbuf_tensor(f"ptil{q}_{p}", [128, 640], BF16, side="right")) for p in range(2)]
                for q in range(2)]
        ptn = [[ET(nc.sbuf_tensor(f"ptn{q}_{p}", [128, 640], BF16, side="right")) for p in range(2)]
               for q in range(2)]
        ptrs = [[ET(nc.sbuf_tensor(f"ptrs{q}_{p}", [128, 640], BF16, side="right")) for p in range(2)]
                for q in range(2)]
        ssc = [[ET(nc.sbuf_tensor(f"ssc{q}_{p}", [128, 640], F32, side="right")) for p in range(2)]
               for q in range(4)]
        nmx2 = ET(nc.sbuf_tensor("nmx2", [128, 8], F32, side="right"))
        zz2 = ET(nc.sbuf_tensor("zz2", [128, 8], F32, side="right"))
        rz2 = [ET(nc.sbuf_tensor(f"rz2_{p}", [128, 2], F32, side="right")) for p in range(4)]
        strail = ET(nc.sbuf_tensor("strail", [128, 2], F32, side="right"))
        ps_s = [ET(nc.psum_tensor(f"ps_s{p}", [128, 1024], F32)) for p in range(2)]
        ps_pt = [ET(nc.psum_tensor(f"ps_pt{p}", [128, 1024], BF16)) for p in range(2)]
        ps_pv = [ET(nc.psum_tensor(f"ps_pv{p}", [64, 512], F32)) for p in range(2)]

        pairs = []
        for t in range(8):
            for h0 in range(0, NQH, 2):
                pairs.append((t, h0))
        NP = len(pairs)
        at_sc, at_scev, at_mx, at_exp, at_tp, at_pv, at_pn = ({} for _ in range(7))
        at_ptev, at_evac, vw2 = ({} for _ in range(3))

        def em_scores(i):
            t, h0 = pairs[i]

            @block.tensor
            def _(tensor, t=t, heads=(h0, h0 + 1), i=i):
                if i >= 1:
                    tensor.wait_ge(semV, at_scev[i - 1])
                for j, h in enumerate(heads):
                    g = h >> 2
                    off = 64 * (h & 1)
                    lhs = qhat[off:off + 64,
                               (h >> 1) * SOWN + 128 * t: (h >> 1) * SOWN + 128 * t + 128]
                    k0 = g * SK + 128 * t
                    tensor.matmul(ps_s[j][:, 0:512], lhs,
                                  khat[off:off + 64, k0:k0 + 512], start=True, stop=True)
                    tensor.matmul(ps_s[j][:, 512:640], lhs,
                                  khat[off:off + 64, k0 + 512:k0 + 640], start=True, stop=True)
                tensor.last.then_inc(semT, 1)
                cT.inc()
            at_sc[i] = cT.v

        def em_v1(i):
            if not i < NP:
                return

            @block.vector
            def _(vector, i=i):
                vector.wait_ge(semT, at_sc[i])
                if i >= 4:
                    vector.wait_ge(semS, at_exp[i - 4])
                t = pairs[i][0]
                pp = i % 4
                ee = None
                for j in range(2):
                    ee = vector.scalar_tensor_tensor(
                        ssc[pp][j][:], ps_s[j][:, 0:640], 1.0,
                        masks[:, 640 * t:640 * t + 640], ALU.mult, ALU.add)
                ee.then_inc(semV, 1)
                cV.inc()
                at_scev[i] = cV.v

        def em_v2(i):
            has_evac = 3 <= i <= NP + 2
            has_max = i < NP
            has_ptev = 2 <= i <= NP + 1
            has_norm = 1 <= i <= NP
            if not (has_evac or has_max or has_ptev or has_norm):
                return

            @block.vector
            def _(vector, i=i, has_evac=has_evac, has_max=has_max,
                  has_ptev=has_ptev, has_norm=has_norm):
                if has_evac:
                    vector.wait_ge(semT, at_pv[i - 3])
                    pj = i - 3
                    t2, h02 = pairs[pj]
                    le = None
                    for j, h in enumerate((h02, h02 + 1)):
                        off = 64 * (h & 1)
                        dst = attnT[off:off + 64,
                                    (h >> 1) * SOWN + 128 * t2: (h >> 1) * SOWN + 128 * t2 + 128]
                        le = vector.tensor_copy(dst, ps_pv[j][:, 0:128])
                    le.then_inc(semV, 1)
                    cV.inc()
                    at_evac[pj] = cV.v
                if has_max:
                    if not has_evac:
                        vector.drain()
                    mm = None
                    pp = i % 4
                    for j in range(2):
                        jo = pp * 2 + j
                        mm = vector.reduce_max(nmx2[:, jo:jo + 1], ssc[pp][j][:],
                                               axis=mybir.AxisListType.X, negate=True)
                    mm.then_inc(semV, 1)
                    cV.inc()
                    at_mx[i] = cV.v
                if has_ptev:
                    vector.wait_ge(semT, at_tp[i - 2])
                    if i >= 3:
                        vector.wait_ge(semT, at_pv[i - 3])
                    pp = (i - 2) % 2
                    for j in range(2):
                        vector.tensor_copy(ptrs[pp][j][:], ps_pt[j][:, 0:640])
                    vector.drain()
                    vector.nop().then_inc(semV, 1)
                    cV.inc()
                    at_ptev[i - 2] = cV.v
                if has_norm:
                    vector.wait_ge(semS, at_exp[i - 1])
                    zo = ((i - 1) % 4) * 2
                    vector.reciprocal(rz2[((i - 1) % 4)][:], zz2[:, zo:zo + 2])
                    vector.drain()
                    vector.nop().then_inc(semV, 1)
                    cV.inc()
                    vw2[i] = cV.v

        EG = 8

        def em_exp_group(idxs):
            @block.scalar
            def _(scalar, idxs=tuple(idxs)):
                for i in idxs:
                    if 1 <= i <= NP:
                        scalar.wait_ge(semV, vw2[i])
                        pp = (i - 1) % 2
                        p4 = (i - 1) % 4
                        for j in range(2):
                            pn = scalar.activation(ptn[pp][j][:], ptil[pp][j][:],
                                                   AF.Copy, scale=rz2[p4][:, j:j + 1])
                        pn.then_inc(semS, 1)
                        cS.inc()
                        assert at_pn[i - 1] == cS.v
                    if i < NP:
                        scalar.wait_ge(semV, at_mx[i])
                        pp = i % 2
                        p4 = i % 4
                        for j in range(2):
                            jo = p4 * 2 + j
                            scalar.activation(ptil[pp][j][:], ssc[p4][j][:], AF.Exp,
                                              bias=nmx2[:, jo:jo + 1], scale=1.0,
                                              accum_out=zz2[:, jo:jo + 1])
                        tr = scalar.activation(strail[:], nmx2[:, 0:2], AF.Exp)
                        tr.then_inc(semS, 1)
                        cS.inc()
                        assert at_exp[i] == cS.v

        def em_tposes(pi):
            @block.tensor
            def _(tensor, pi=pi):
                tensor.wait_ge(semS, at_pn[pi])
                tensor.wait_ge(semV, vw2[pi + 1])
                pp = pi % 2
                for j in range(2):
                    for u in range(5):
                        tensor.transpose(ps_pt[j][:, u * 128:(u + 1) * 128],
                                         ptn[pp][j][:, u * 128:(u + 1) * 128], eye[:, :])
                tensor.last.then_inc(semT, 1)
                cT.inc()
            at_tp[pi] = cT.v

        def em_pv(pj):
            t, h0 = pairs[pj]

            @block.tensor
            def _(tensor, t=t, heads=(h0, h0 + 1), pj=pj):
                tensor.wait_ge(semV, at_ptev[pj])
                if pj >= 1:
                    tensor.wait_ge(semV, at_evac[pj - 1])
                pp = pj % 2
                for j, h in enumerate(heads):
                    g = h >> 2
                    for u in range(5):
                        ktile = t + u
                        tensor.matmul(ps_pv[j][:, 0:128],
                                      vst[:, ktile * 512 + 64 * g: ktile * 512 + 64 * g + 64],
                                      ptrs[pp][j][:, u * 128:(u + 1) * 128],
                                      start=(u == 0), stop=(u == 4))
                tensor.last.then_inc(semT, 1)
                cT.inc()
            at_pv[pj] = cT.v

        _run = cS.v
        for i in range(NP + 1):
            if 1 <= i <= NP:
                _run += 1
                at_pn[i - 1] = _run
            if i < NP:
                _run += 1
                at_exp[i] = _run
        cS_end_p3 = _run
        exp_pend = []
        for i in range(NP + 3):
            if i < NP:
                em_scores(i)
            if 2 <= i <= NP + 1:
                em_tposes(i - 2)
            em_v1(i)
            em_v2(i)
            if i <= NP:
                exp_pend.append(i)
                if len(exp_pend) == EG or i == NP:
                    em_exp_group(exp_pend)
                    exp_pend = []
            if 2 <= i <= NP + 1:
                em_pv(i - 2)
        assert cS.v == cS_end_p3, (cS.v, cS_end_p3)

        barrier()
        if DEBUG_TAPS:
            @block.sync
            def _(sync):
                for rt in range(16):
                    sync.dma_start(out=at_dram[128 * rt:128 * (rt + 1), :],
                                   in_=attnT[:, rt * SOWN:(rt + 1) * SOWN]).then_inc(semO, 16)
                    cO.inc(16)
        at.close()
        maybe_stop(3)

        nc.leave_named_scope("P3_attn", _sid_P3_attn, False)
        _sid_P4_wout = nc.enter_named_scope("P4_wout", False)[0]
        wo = contextlib.ExitStack()
        _stacks.append(wo)
        EW = wo.enter_context
        wocol = [EW(nc.sbuf_tensor(f"wocol{p}", [128, 16 * 128], BF16, side="right")) for p in range(2)]
        swo = [EW(nc.sbuf_tensor(f"swo{p}", [128, 512], F32, side="right")) for p in range(2)]
        ps_wo = EW(nc.psum_tensor("ps_wo", [128, 1024], F32))

        wo_load, wo_mm, wo_ev, wo_st = {}, {}, {}, {}
        for dt_ in range(16):
            @block.gpsimd
            def _(gpsimd, dt_=dt_):
                if dt_ >= 2:
                    gpsimd.wait_ge(semT, wo_mm[(dt_ - 2, 1)])
                gpsimd.dma_start(
                    out=_s3(wocol[dt_ % 2][:], 16),
                    in_=_m3(woutT[:, 128 * dt_:128 * (dt_ + 1)], 16)).then_inc(semG, 16)
                cG.inc(16)
            wo_load[dt_] = cG.v

            for c2 in range(2):
                par = c2
                psl = slice(par * 512, (par + 1) * 512)

                @block.tensor
                def _(tensor, dt_=dt_, c2=c2, psl=psl):
                    tensor.wait_ge(semG, wo_load[dt_])
                    prev = (dt_ - 1, c2)
                    if prev in wo_ev:
                        tensor.wait_ge(semV, wo_ev[prev])
                    for kt in range(16):
                        tensor.matmul(ps_wo[:, psl],
                                      wocol[dt_ % 2][:, kt * 128:(kt + 1) * 128],
                                      attnT[:, kt * SOWN + 512 * c2: kt * SOWN + 512 * c2 + 512],
                                      start=(kt == 0), stop=(kt == 15))
                    tensor.last.then_inc(semT, 1)
                    cT.inc()
                wo_mm[(dt_, c2)] = cT.v

                @block.vector
                def _(vector, dt_=dt_, c2=c2, psl=psl):
                    vector.wait_ge(semT, wo_mm[(dt_, c2)])
                    prev = (dt_ - 1, c2)
                    if prev in wo_st:
                        vector.wait_ge(semO, wo_st[prev])
                    vector.tensor_copy(swo[c2][:], ps_wo[:, psl])
                    vector.last.then_inc(semV, 1)
                    cV.inc()
                wo_ev[(dt_, c2)] = cV.v

                @block.sync
                def _(sync, dt_=dt_, c2=c2):
                    sync.wait_ge(semV, wo_ev[(dt_, c2)])
                    sync.dma_start(out=sT_dram[128 * dt_:128 * (dt_ + 1), 512 * c2:512 * (c2 + 1)],
                                   in_=swo[c2][:]).then_inc(semO, 16)
                    cO.inc(16)
                wo_st[(dt_, c2)] = cO.v

        barrier()
        wo.close()
        atn.close()
        att.close()
        maybe_stop(4)

        nc.leave_named_scope("P4_wout", _sid_P4_wout, False)
        _sid_P5_gateup = nc.enter_named_scope("P5_gateup", False)[0]
        gsc = contextlib.ExitStack()
        _stacks.append(gsc)
        g_store = gsc.enter_context(nc.sbuf_tensor("g_store", [128, 44 * SOWN], BF16))
        gu = contextlib.ExitStack()
        _stacks.append(gu)
        EG = gu.enter_context
        dec = [EG(nc.sbuf_tensor(f"dec{p}", [128, 16 * 512], BF16, side="right")) for p in range(2)]
        pkb = [EG(nc.sbuf_tensor(f"pkb{p}", [128, 4 * 512], U16, side="right")) for p in range(2)]
        scb = [EG(nc.sbuf_tensor(f"scb{p}", [128, 4 * 512], F16, side="right")) for p in range(2)]
        dqv = [EG(nc.sbuf_tensor(f"dqv{k}", [128, 2048], U16, side="right")) for k in range(4)]
        dqs = [EG(nc.sbuf_tensor(f"dqs{k}", [128, 2048], U16, side="right")) for k in range(4)]
        scq = EG(nc.sbuf_tensor("scq", [128, 2048], F16, side="right"))
        ps_gu = EG(nc.psum_tensor("ps_gu", [128, 1024], F32))

        gu_load, gu_dq, gu_mm, gu_ev = ({} for _ in range(4))

        for oc in range(22):
            pb = oc % 2
            is_gate_oc = 512 * (oc + 1) <= II

            @block.sync
            def _(sync, oc=oc, pb=pb):
                if oc >= 2:
                    sync.wait_ge(semV, gu_dq[oc - 2])
                sync.dma_start(out=_s3(pkb[pb][:], 4),
                               in_=_m3(pk16[:, 512 * oc:512 * (oc + 1)], 4)).then_inc(semL, 16)
                cL.inc(16)
                sync.dma_start(out=_s3(scb[pb][:], 4),
                               in_=_m3(screp[:, 512 * oc:512 * (oc + 1)], 4)).then_inc(semL, 16)
                cL.inc(16)
            gu_load[oc] = cL.v

            @block.vector
            def _(vector, oc=oc, pb=pb):
                vector.wait_ge(semL, gu_load[oc])
                if oc >= 2:
                    vector.wait_ge(semT, gu_mm[(oc - 2, 3, 1)])
                W = pkb[pb][:, 0:2048]
                vector.tensor_scalar(scq[:], scb[pb][:, 0:2048], 16384.0, None, ALU.mult)
                for kk in range(4):
                    sh = 9 - 4 * kk
                    if sh >= 0:
                        vector.tensor_scalar(dqv[kk][:], W, sh, 7 << 9,
                                             ALU.logical_shift_left, ALU.bitwise_and)
                    else:
                        vector.tensor_scalar(dqv[kk][:], W, -sh, 7 << 9,
                                             ALU.logical_shift_right, ALU.bitwise_and)
                for kk in range(4):
                    vector.tensor_scalar(dqs[kk][:], W, 4 * kk + 3, 15,
                                         ALU.logical_shift_right, ALU.logical_shift_left)
                for kk in range(4):
                    vector.tensor_tensor(dqv[kk][:], dqv[kk][:], dqs[kk][:], ALU.add)
                for kk in range(4):
                    vector.tensor_tensor(
                        dec[oc % 2][:, 2048 * kk: 2048 * (kk + 1)],
                        dqv[kk][:].bitcast(F16), scq[:], ALU.mult)
                vector.drain()
                vector.nop().then_inc(semV, 1)
                cV.inc()
            gu_dq[oc] = cV.v

            if is_gate_oc:
                for osub in range(4):
                    for c2 in range(2):
                        gu_ev[(oc, osub, c2)] = (semS, cS.v + osub * 2 + c2 + 1)

            for osub in range(4):
                o_glob = 512 * oc + 128 * osub
                for c2 in range(2):
                    par = c2
                    psl = slice(par * 512, (par + 1) * 512)

                    @block.tensor
                    def _(tensor, oc=oc, osub=osub, c2=c2, psl=psl):
                        tensor.wait_ge(semV, gu_dq[oc])
                        seqpos = osub * 2 + c2
                        if seqpos >= 2:
                            pk_ = (oc, (seqpos - 2) // 2, (seqpos - 2) % 2)
                        elif oc >= 1:
                            pk_ = (oc - 1, (seqpos + 6) // 2, (seqpos + 6) % 2)
                        else:
                            pk_ = None
                        if pk_ is not None:
                            sh, sv = gu_ev[pk_]
                            tensor.wait_ge(sh, sv)
                        for ht in range(16):
                            tensor.matmul(
                                ps_gu[:, psl],
                                dec[oc % 2][:, ht * 512 + 128 * osub: ht * 512 + 128 * osub + 128],
                                xnT[:, ht * SOWN + 512 * c2: ht * SOWN + 512 * c2 + 512],
                                start=(ht == 0), stop=(ht == 15))
                        tensor.last.then_inc(semT, 1)
                        cT.inc()
                    gu_mm[(oc, osub, c2)] = cT.v

                    if o_glob < II:
                        pass
                    else:
                        ut = (o_glob - II) // 128

                        @block.vector
                        def _(vector, oc=oc, osub=osub, c2=c2, psl=psl, ut=ut):
                            vector.wait_ge(semT, gu_mm[(oc, osub, c2)])
                            sl = slice(ut * SOWN + 512 * c2, ut * SOWN + 512 * c2 + 512)
                            vector.tensor_tensor(g_store[:, sl], g_store[:, sl],
                                                 ps_gu[:, psl], ALU.mult)
                            vector.last.then_inc(semV, 1)
                            cV.inc()
                        gu_ev[(oc, osub, c2)] = (semV, cV.v)

            if is_gate_oc:
                @block.scalar
                def _(scalar, oc=oc):
                    for osub in range(4):
                        for c2 in range(2):
                            gt = (512 * oc + 128 * osub) // 128
                            psl = slice(c2 * 512, (c2 + 1) * 512)
                            scalar.wait_ge(semT, gu_mm[(oc, osub, c2)])
                            aa = scalar.activation(
                                g_store[:, gt * SOWN + 512 * c2: gt * SOWN + 512 * c2 + 512],
                                ps_gu[:, psl], AF.Silu)
                            aa.then_inc(semS, 1)
                            cS.inc()
                            assert gu_ev[(oc, osub, c2)] == (semS, cS.v)

        barrier()
        if DEBUG_TAPS:
            @block.sync
            def _(sync):
                for gt in range(44):
                    sync.dma_start(out=h_dram[128 * gt:128 * (gt + 1), :],
                                   in_=g_store[:, gt * SOWN:(gt + 1) * SOWN]).then_inc(semO, 16)
                    cO.inc(16)
        gu.close()
        xn_stack.close()
        maybe_stop(5)

        nc.leave_named_scope("P5_gateup", _sid_P5_gateup, False)
        _sid_P7_down = nc.enter_named_scope("P7_down", False)[0]
        dn = contextlib.ExitStack()
        _stacks.append(dn)
        EN = dn.enter_context
        wdc = [EN(nc.sbuf_tensor(f"wdc{p}", [128, 44 * 128], BF16, side="right")) for p in range(2)]
        ysq = [EN(nc.sbuf_tensor(f"ysq{p}", [128, 512], BF16, side="right")) for p in range(2)]
        y_store = EN(nc.sbuf_tensor("y_store", [128, 16 * 512], F32, side="right"))
        srms2 = EN(nc.sbuf_tensor("srms2", [128, 512], F32, side="right"))
        rb2 = EN(nc.sbuf_tensor("rb2", [128, 512], F32, side="right"))
        sxt = [EN(nc.sbuf_tensor(f"sxt{p}", [128, 512], F32, side="right")) for p in range(4)]
        ssT = [EN(nc.sbuf_tensor(f"ssT{p}", [128, 512], F32, side="right")) for p in range(4)]
        comb = EN(nc.sbuf_tensor("comb", [128, 512], F32, side="right"))
        outb = [EN(nc.sbuf_tensor(f"outb{p}", [128, 512], F32, side="right")) for p in range(2)]
        ps_dn = EN(nc.psum_tensor("ps_dn", [128, 1024], F32))
        ps_ss2 = EN(nc.psum_tensor("ps_ss2", [128, 512], F32))

        dn_wload, dn_mm, dn_ysq, dn_ssmm, dn_ev, dn_ysq_s = ({} for _ in range(6))
        dn_xs_load, dn_comb, dn_ost = ({} for _ in range(3))

        for c2 in range(2):
            for ht in range(16):
                sq_ = c2 * 16 + ht
                wb = sq_ % 2
                pb = ht % 2
                psl = slice(pb * 512, (pb + 1) * 512)

                @block.gpsimd
                def _(gpsimd, c2=c2, ht=ht, wb=wb, sq_=sq_):
                    prevk = (sq_ - 2) if sq_ >= 2 else None
                    if prevk is not None:
                        gpsimd.wait_ge(semT, dn_mm[(prevk // 16, prevk % 16)])
                    gpsimd.dma_start(
                        out=_s3(wdc[wb][:], 44),
                        in_=_m3(wdn_dram[:, 128 * ht:128 * (ht + 1)], 44)).then_inc(semG, 16)
                    cG.inc(16)
                dn_wload[(c2, ht)] = cG.v

                @block.tensor
                def _(tensor, c2=c2, ht=ht, wb=wb, psl=psl):
                    tensor.wait_ge(semG, dn_wload[(c2, ht)])
                    prevk = (c2, ht - 2) if ht >= 2 else ((c2 - 1, 14 + ht) if c2 > 0 else None)
                    if prevk is not None and prevk in dn_ev:
                        tensor.wait_ge(semV, dn_ev[prevk])
                    for kt in range(44):
                        tensor.matmul(ps_dn[:, psl], wdc[wb][:, kt * 128:(kt + 1) * 128],
                                      g_store[:, kt * SOWN + 512 * c2: kt * SOWN + 512 * c2 + 512],
                                      start=(kt == 0), stop=(kt == 43))
                    tensor.last.then_inc(semT, 1)
                    cT.inc()
                dn_mm[(c2, ht)] = cT.v

                @block.scalar
                def _(scalar, c2=c2, ht=ht, psl=psl, pb=pb):
                    scalar.wait_ge(semT, dn_mm[(c2, ht)])
                    if ht > 1:
                        scalar.wait_ge(semT, dn_ssmm[(c2, ht - 2)])
                    scalar.activation(ysq[pb][:], ps_dn[:, psl], AF.Square)
                    scalar.last.then_inc(semS, 1)
                    cS.inc()
                dn_ysqs = cS.v

                @block.vector
                def _(vector, c2=c2, ht=ht, psl=psl):
                    vector.wait_ge(semT, dn_mm[(c2, ht)])
                    vector.tensor_copy(y_store[:, ht * 512:(ht + 1) * 512], ps_dn[:, psl])
                    vector.last.then_inc(semV, 1)
                    cV.inc()
                dn_ysq[(c2, ht)] = cV.v
                dn_ev[(c2, ht)] = dn_ysq[(c2, ht)]

                @block.tensor
                def _(tensor, c2=c2, ht=ht, dn_ysqs=dn_ysqs, pb=pb):
                    tensor.wait_ge(semS, dn_ysqs)
                    tensor.wait_ge(semV, dn_ysq[(c2, ht)])
                    tensor.matmul(ps_ss2[:, :], ones[:, :], ysq[pb][:],
                                  start=(ht == 0), stop=(ht == 15))
                    tensor.last.then_inc(semT, 1)
                    cT.inc()
                dn_ssmm[(c2, ht)] = cT.v

            @block.scalar
            def _(scalar, c2=c2):
                scalar.wait_ge(semT, dn_ssmm[(c2, 15)])
                scalar.activation(srms2[:], ps_ss2[:, :], AF.Sqrt, bias=c_feps, scale=1.0 / D)
                scalar.last.then_inc(semS, 1)
                cS.inc()
            rms_done = cS.v

            @block.vector
            def _(vector, c2=c2, rms_done=rms_done):
                vector.wait_ge(semS, rms_done)
                vector.reciprocal(rb2[:], srms2[:])
                vector.drain()
                for ht in range(16):
                    sl = slice(ht * 512, (ht + 1) * 512)
                    vector.tensor_tensor(y_store[:, sl], y_store[:, sl], rb2[:], ALU.mult)
                vector.last.then_inc(semV, 1)
                cV.inc()
            ynorm_done = cV.v

            @block.scalar
            def _(scalar, c2=c2, ynorm_done=ynorm_done):
                scalar.wait_ge(semV, ynorm_done)
                for ht in range(16):
                    sl = slice(ht * 512, (ht + 1) * 512)
                    scalar.activation(y_store[:, sl], y_store[:, sl], AF.Silu)
                scalar.last.then_inc(semS, 1)
                cS.inc()
            g_done = cS.v

            for ht in range(16):
                pb = ht % 2

                @block.sync
                def _(sync, c2=c2, ht=ht):
                    prevk = (c2, ht - 4) if ht >= 4 else ((c2 - 1, 12 + ht) if c2 > 0 else None)
                    if prevk is not None and prevk in dn_comb:
                        sync.wait_ge(semV, dn_comb[prevk])
                    sync.dma_start(out=sxt[ht % 4][:],
                                   in_=xT[128 * ht:128 * (ht + 1),
                                          HALO + 512 * c2: HALO + 512 * c2 + 512]
                                   ).then_inc(semL, 16)
                    cL.inc(16)
                    sync.dma_start(out=ssT[ht % 4][:],
                                   in_=sT_dram[128 * ht:128 * (ht + 1), 512 * c2:512 * (c2 + 1)]
                                   ).then_inc(semL, 16)
                    cL.inc(16)
                dn_xs_load[(c2, ht)] = cL.v

                @block.vector
                def _(vector, c2=c2, ht=ht, pb=pb, g_done=g_done):
                    vector.wait_ge(semS, g_done)
                    vector.wait_ge(semL, dn_xs_load[(c2, ht)])
                    prevk = (c2, ht - 2) if ht >= 2 else ((c2 - 1, 14 + ht) if c2 > 0 else None)
                    if prevk is not None and prevk in dn_ost:
                        vector.wait_ge(semO, dn_ost[prevk])
                    sl = slice(ht * 512, (ht + 1) * 512)
                    vector.tensor_tensor(comb[:], ssT[ht % 4][:], y_store[:, sl], ALU.mult)
                    vector.drain()
                    vector.tensor_tensor(outb[pb][:], sxt[ht % 4][:], comb[:], ALU.add)
                    vector.last.then_inc(semV, 1)
                    cV.inc()
                dn_comb[(c2, ht)] = cV.v

                @block.sync
                def _(sync, c2=c2, ht=ht, pb=pb):
                    sync.wait_ge(semV, dn_comb[(c2, ht)])
                    sync.dma_start(out=o_out[128 * ht:128 * (ht + 1), 512 * c2:512 * (c2 + 1)],
                                   in_=outb[pb][:]).then_inc(semO, 16)
                    cO.inc(16)
                dn_ost[(c2, ht)] = cO.v

        @block.sync
        def _(sync):
            sync.wait_ge(semO, cO.v)

        nc.leave_named_scope("P7_down", _sid_P7_down, False)
        dn.close()
        gsc.close()

    return nc


def kernel(L0_attn_x, L0_attn_w_qkv, L0_attn_w_out, L0_attn_q_norm_w, L0_attn_k_norm_w,
           L0_ffn_gate_up_packed, L0_ffn_gate_up_scales, L0_ffn_down_packed,
           L0_ffn_down_scales):
    SD, fD = _perm_d()
    SI, fI = _perm_i()

    x = np.asarray(L0_attn_x, dtype=np.float32)
    wqkv = np.asarray(L0_attn_w_qkv, dtype=np.float32)
    wout = np.asarray(L0_attn_w_out, dtype=np.float32)
    qnw = np.asarray(L0_attn_q_norm_w, dtype=np.float32)
    knw = np.asarray(L0_attn_k_norm_w, dtype=np.float32)
    gup = np.asarray(L0_ffn_gate_up_packed)
    gus = np.asarray(L0_ffn_gate_up_scales)
    dnp_ = np.asarray(L0_ffn_down_packed)
    dns = np.asarray(L0_ffn_down_scales)

    wqkvT_st = np.ascontiguousarray(wqkv.T[SD, :])
    woutT_st = np.ascontiguousarray(wout.T[:, SD])
    wqpp_st = np.ascontiguousarray(np.tile(qnw, 2)[:, None])
    wkpp_st = np.ascontiguousarray(np.tile(knw, 2)[:, None])

    cols = np.concatenate([SI, II + SI])
    pk16_st = np.ascontiguousarray(gup.view('<u2').reshape(2 * II, 512)[cols, :].T)
    screp_st = np.ascontiguousarray(np.repeat(gus.astype(np.float16), 32, axis=0)[:, cols])
    dpk16_st = np.ascontiguousarray(dnp_.view('<u2').reshape(D, 1408)[SD, :].T)
    dscrep_st = np.ascontiguousarray(np.repeat(dns.astype(np.float16), 32, axis=0)[:, SD])

    eye_st = np.eye(128, dtype=np.float32).astype(bfdt)
    bd_st = np.zeros((128, 128), np.float32)
    bd_st[:64, :64] = 1.0
    bd_st[64:, 64:] = 1.0
    bd_st = bd_st.astype(bfdt)
    ones_st = np.ones((128, 128), np.float32).astype(bfdt)
    consts_st = np.zeros((128, 8), np.float32)
    consts_st[:, 0] = 0.5
    consts_st[:, 1] = 64.0 * 1e-6
    consts_st[:, 2] = 1e-6
    consts_st[:, 3] = 1e-5
    consts_st[:, 4] = 1e-6

    i_ = np.arange(128)[:, None]
    jj = np.arange(640)[None, :]
    win = (jj > i_) & (jj <= i_ + 512)
    mask_by_half = {}
    for half in (0, 1):
        m = np.zeros((8, 128, 640), np.float32)
        for t in range(8):
            ok = win.copy()
            if half == 0:
                ok = ok & ((128 * t + jj) >= 512)
            m[t] = np.where(ok, 0.0, NEG)
        mask_by_half[half] = np.ascontiguousarray(
            m.transpose(1, 0, 2).reshape(128, 8 * 640)).astype(bfdt)

    invf = 1.0 / (10000.0 ** (np.arange(0, HD, 2, dtype=np.float64) / HD))
    rope_by_half = {}
    for half in (0, 1):
        q0 = SOWN * half
        pos = np.maximum(q0 - HALO + np.arange(SK, dtype=np.float64), 0.0)
        ang = pos[None, :] * invf[:, None]
        ang2 = np.concatenate([ang, ang], axis=0)
        cos = np.concatenate([np.cos(ang2)] * 2, axis=0).astype(bfdt)
        sin = np.concatenate([np.sin(ang2)] * 2, axis=0).astype(bfdt)
        rope_by_half[half] = (np.ascontiguousarray(cos), np.ascontiguousarray(sin))

    in_maps = []
    for c in range(8):
        b, half = c // 2, c % 2
        q0 = SOWN * half
        xb = x[b].T[SD, :]
        xs = np.zeros((D, SK), np.float32)
        lo = q0 - HALO
        src_lo = max(lo, 0)
        xs[:, src_lo - lo:] = xb[:, src_lo:q0 + SOWN]
        cos_st, sin_st = rope_by_half[half]
        in_maps.append(dict(
            xT=np.ascontiguousarray(xs), wqkvT=wqkvT_st, woutT=woutT_st,
            wqpp=wqpp_st, wkpp=wkpp_st, pk16=pk16_st, screp=screp_st,
            dpk16=dpk16_st, dscrep=dscrep_st, masks_in=mask_by_half[half],
            cos_in=cos_st, sin_in=sin_st, eye_in=eye_st, bd_in=bd_st,
            ones_in=ones_st, consts_in=consts_st,
        ))

    nc = build_nc()
    res = run_bass_kernel_spmd(nc, in_maps, core_ids=list(range(8)))

    out = np.empty((4, 2048, D), np.float32)
    for c in range(8):
        b, half = c // 2, c % 2
        o = res.results[c]["o_out"]
        out[b, SOWN * half:SOWN * (half + 1), :] = o[fD, :].T
    if DEBUG_TAPS:
        kernel.last_results = res
    return out
